# version 04 (final)
# speedup vs baseline: 1.4636x; 1.2791x over previous
import contextlib
import math
import numpy as np

import concourse.bass as bass
import concourse.mybir as mybir
from concourse.bass import IndirectOffsetOnAxis
from concourse.bass_utils import run_bass_kernel_spmd

F32 = mybir.dt.float32
BF16 = mybir.dt.bfloat16
U32 = mybir.dt.uint32
I32 = mybir.dt.int32
AF = mybir.ActivationFunctionType
ALU = mybir.AluOpType
AX = mybir.AxisListType

N_CORES = 8
T = 4096
D = 1024

C_GELU = 0.044715
S_GELU = 0.7978845608028654


class Sem:
    __slots__ = ("h", "n", "serial")

    def __init__(self, h, serial=False):
        self.h = h
        self.n = 0
        self.serial = serial


class Prog:

    ENGINES = ("tensor", "vector", "scalar", "gpsimd", "sync")
    CHAINED = ("vector", "scalar", "gpsimd")

    def __init__(self, nc, ctx):
        self.nc = nc
        self.ctx = ctx
        self.q = {e: [] for e in self.ENGINES}
        self.esem = {e: self.sem("ch_" + e) for e in self.CHAINED}
        self.mark = {}

    def sem(self, name, serial=False):
        return Sem(self.ctx.enter_context(self.nc.semaphore(name)), serial)

    def emit(self, eng, fn, waits=(), inc=None, inc_n=1):
        waits = list(waits)
        if inc is None and eng in self.CHAINED:
            inc = self.esem[eng]
        if inc is not None:
            if inc.serial and inc.n > 0:
                waits.append((inc, inc.n))
            inc.n += inc_n
        if eng in self.CHAINED:
            ch = self.esem[eng]
            prev = ch.n - inc_n if inc is ch else ch.n
            if prev > 0 and not any(s is ch and v >= prev for (s, v) in waits):
                waits.append((ch, prev))
        w = []
        for s, v in waits:
            v = int(v)
            if v <= 0:
                continue
            k = (eng, id(s))
            if self.mark.get(k, 0) >= v:
                continue
            self.mark[k] = v
            w.append((s.h, v))
        ih = inc.h if inc is not None else None

        def run(e, fn=fn, w=w, ih=ih, inc_n=inc_n):
            for h, v in w[:-1]:
                e.wait_ge(h, v)
            ins = fn(e)
            if w:
                ins._wait_ge(w[-1][0], w[-1][1])
            if ih is not None:
                ins.then_inc(ih, inc_n)
        self.q[eng].append(run)
        return inc.n if inc is not None else None

    def wait(self, eng, waits):
        w = []
        for s, v in waits:
            v = int(v)
            if v <= 0:
                continue
            k = (eng, id(s))
            if self.mark.get(k, 0) >= v:
                continue
            self.mark[k] = v
            w.append((s.h, v))

        def run(e, w=w):
            for h, v in w:
                e.wait_ge(h, v)
        self.q[eng].append(run)

    def mm_group(self, mats, waits=(), inc=None, inc_n=1):
        n = len(mats)
        out = None
        for i, fn in enumerate(mats):
            out = self.emit("tensor", fn,
                            waits=waits if i == 0 else (),
                            inc=inc if i == n - 1 else None, inc_n=inc_n)
        return out

    def finalize(self):
        nc = self.nc
        with nc.Block() as block:
            if self.q["sync"]:
                block.sync(lambda e: [f(e) for f in self.q["sync"]])
            if self.q["tensor"]:
                block.tensor(lambda e: [f(e) for f in self.q["tensor"]])
            if self.q["vector"]:
                block.vector(lambda e: [f(e) for f in self.q["vector"]])
            if self.q["scalar"]:
                block.scalar(lambda e: [f(e) for f in self.q["scalar"]])
            if self.q["gpsimd"]:
                block.gpsimd(lambda e: [f(e) for f in self.q["gpsimd"]])


def _np_bf16(a):
    import ml_dtypes
    return np.asarray(a, dtype=ml_dtypes.bfloat16)


_CONST_DEFS = {
    "identbf": lambda: _np_bf16(np.eye(128)),
    "identf": lambda: np.eye(128, dtype=np.float32),
    "ones512": lambda: _np_bf16(np.ones((1, 512))),
    "ones128": lambda: _np_bf16(np.ones((1, 128))),
    "o16r": lambda: np.ones((1, 16), dtype=np.float32),
    "o16c": lambda: np.ones((16, 1), dtype=np.float32),
    "o128c": lambda: _np_bf16(np.ones((128, 1))),
    "o16cb": lambda: _np_bf16(np.ones((16, 1))),
    "o1r": lambda: np.ones((1, 128), dtype=np.float32),
    "ones64": lambda: _np_bf16(np.ones((64, 1))),
    "ecst512": lambda: np.broadcast_to(
        np.arange(8, dtype=np.float32) * 512.0, (128, 8)).copy(),
    "iota128": lambda: np.arange(128, dtype=np.float32)[:, None].copy(),
    "tri128": lambda: np.triu(np.ones((128, 128), dtype=np.float32)).copy(),
    "sel127": lambda: np.eye(128, 1, k=-127, dtype=np.float32).copy(),
}


class Host:

    def __init__(self, standalone=True):
        self.standalone = standalone
        self.nc = bass.Bass("TRN2", target_bir_lowering=False, debug=False)
        self._stack = contextlib.ExitStack()
        self.p = Prog(self.nc, self._stack)
        self.sems = {}
        self.consts = {}
        self.cast_tags = {}
        self.scratches = {}
        self.prefix = ""
        self._x_cur = None
        self._x_bufs = None
        self._xo_final = None
        self._section_n = 0
        self._n_sections = 1

    def u(self, name):
        return f"{name}_s{self._section_n}"

    def sem(self, name, serial=False):
        if name not in self.sems:
            self.sems[name] = self.p.sem(name, serial=serial)
        return self.sems[name]

    def const(self, key):
        if key not in self.consts:
            arr = _CONST_DEFS[key]()
            dk = key + "_c"
            if dk not in self.scratches:
                self.scratches[dk] = self.nc.inline_tensor(arr, name=dk).ap()
            d = self.scratches[dk]
            sb = self._stack.enter_context(
                self.nc.sbuf_tensor(key + "_sb", list(arr.shape),
                                    BF16 if arr.dtype != np.float32 else F32))
            s_w = self.sem("s_w", serial=True)
            self.p.emit("sync", lambda e, d=d, sb=sb: e.dma_start(out=sb[:], in_=d),
                        inc=s_w, inc_n=16)
            self.consts[key] = sb
        return self.consts[key]

    def sconst(self, key, ctx):
        self._sc_n = getattr(self, "_sc_n", 0) + 1
        arr = _CONST_DEFS[key]()
        dk = key + "_c"
        if dk not in self.scratches:
            self.scratches[dk] = self.nc.inline_tensor(arr, name=dk).ap()
        d = self.scratches[dk]
        sb = ctx.enter_context(
            self.nc.sbuf_tensor(f"{key}_ss{self._sc_n}", list(arr.shape),
                                BF16 if arr.dtype != np.float32 else F32))
        s_w = self.sem("s_w", serial=True)
        self.p.emit("sync", lambda e, d=d, sb=sb: e.dma_start(out=sb[:], in_=d),
                    inc=s_w, inc_n=16)
        return sb

    def inp(self, name, shape, dtype):
        full = self.prefix + name
        if full not in self.scratches:
            self.scratches[full] = self.nc.dram_tensor(
                full, shape, dtype, kind="ExternalInput").ap()
        return self.scratches[full]

    def scratch(self, name, shape, dtype):
        if name not in self.scratches:
            self.scratches[name] = self.nc.dram_tensor(
                name, shape, dtype, kind="Internal").ap()
        return self.scratches[name]

    def x_in(self):
        if self.standalone:
            return self.nc.dram_tensor("x", [T, D], F32, kind="ExternalInput").ap()
        return self._x_cur

    def x_out(self):
        if self.standalone:
            return self.nc.dram_tensor("xo", [T, D], F32, kind="ExternalOutput").ap()
        self._io_n = getattr(self, "_io_n", 0) + 1
        if self._io_n == self._n_sections:
            ret = self._xo_final
        else:
            ret = self._x_bufs[self._io_n % 2]
        self._x_cur = ret
        return ret

    def filter_cast(self, tag, jobs):
        if tag in self.cast_tags:
            return []
        return jobs

    def cast_milestone(self, tag, s_cast):
        if tag not in self.cast_tags:
            self.cast_tags[tag] = s_cast.n
        return self.cast_tags[tag]

    @contextlib.contextmanager
    def section(self, tag):
        self._section_n += 1
        if not self.standalone and self._section_n > 1:
            allsems = list(self.sems.values()) + list(self.p.esem.values())
            for eng in Prog.ENGINES:
                self.p.wait(eng, [(s, s.n) for s in allsems])
        with contextlib.ExitStack() as ctx:
            yield ctx, self.p

    def finish(self):
        self.p.finalize()
        self._stack.close()
        return self.nc


def build_ffn0(host=None):
    own = host is None
    if own:
        host = Host(standalone=True)
    nc = host.nc
    host.prefix = "f0_"
    x_d = host.x_in()
    r_d = host.inp("r", [D, 8], F32)
    w1_d = host.inp("w1", [8, D, 4096], F32)
    b1_d = host.inp("b1", [8, 4096], F32)
    w2_d = host.inp("w2", [8, 4096, D], F32)
    b2_d = host.inp("b2", [8, D], F32)
    xo_d = host.x_out()
    w1b_d = host.scratch("f0w1b", [8, 4, 128, 8192], BF16)
    w2b_d = host.scratch("f0w2b", [8, 4, 128, 8192], BF16)
    b1b_d = host.scratch("f0b1b", [8, 4096], BF16)
    b2b_d = host.scratch("f0b2b", [8, D], BF16)
    lnf0_d = host.scratch("f0lnd", [T + 128, D], BF16)
    slot_d = host.scratch("f0slot", [8 * 512 + 128, 4], F32)
    acc2_d = [[host.scratch(f"f0acc{st_}{c_}", [2 * 1024 + 128, D], F32)
               for c_ in range(2)] for st_ in range(4)]
    ident = host.const("identbf")
    identf = host.const("identf")
    ones512 = host.const("ones512")
    ones128 = host.const("ones128")

    ST = 1024
    NST = T // ST
    NTT = ST // 128
    NE, NHC = 8, 4
    NHB = 8
    CAP = 512
    NCT = CAP // 128

    with host.section("ffn0") as (ctx, p):
        E = ctx.enter_context
        o1r = host.const("o1r")
        ecst = host.sconst("ecst512", ctx)
        iota128 = host.sconst("iota128", ctx)
        tri = host.sconst("tri128", ctx)
        sel127 = host.sconst("sel127", ctx)
        lnb_sb = E(nc.sbuf_tensor(host.u("lnb_sb"), [128, D], BF16))
        glnfm = [E(nc.sbuf_tensor(host.u(f"glnfm{i}"), [128, 8 * 512], BF16))
                 for i in range(2)]
        eacc = E(nc.sbuf_tensor(host.u("eacc"), [128, NCT * D], F32))
        accin2 = [eacc[:, 0:2 * D], eacc[:, 2 * D:4 * D]]
        gt = [E(nc.sbuf_tensor(host.u(f"gt{i}"), [128, D], BF16)) for i in range(4)]
        sc_sb = [E(nc.sbuf_tensor(host.u(f"scsb{i}"), [128, D], F32)) for i in range(3)]
        out_sb = [sc_sb[0][:], sc_sb[1][:]]
        w1f = E(nc.sbuf_tensor(host.u("w1f"), [128, 2048], F32))
        w2f = E(nc.sbuf_tensor(host.u("w2f"), [128, 2048], F32))
        bf32 = eacc[0:8, 0:2048]
        bfb = glnfm[0][0:8, 0:2048]
        maskA_st = E(nc.sbuf_tensor(host.u("maskA_st"), [128, NTT * 8], F32))
        maskT_st = E(nc.sbuf_tensor(host.u("maskT_st"), [128, NTT * 8], F32))
        ptm = E(nc.sbuf_tensor(host.u("ptm"), [128, 8], F32))
        offm = E(nc.sbuf_tensor(host.u("offm"), [128, 8], F32))
        tmp8 = E(nc.sbuf_tensor(host.u("tmp8"), [128, 8], F32))
        ovf8 = E(nc.sbuf_tensor(host.u("ovf8"), [128, 8], F32))
        offc = E(nc.sbuf_tensor(host.u("offc"), [128, 2], F32))
        offi = E(nc.sbuf_tensor(host.u("offi"), [128, 2], I32))
        viof = E(nc.sbuf_tensor(host.u("viof"), [128, 1], F32))
        big128 = E(nc.sbuf_tensor(host.u("big128"), [128, 128], F32))
        onesf8 = E(nc.sbuf_tensor(host.u("onesf8"), [128, 8], F32))
        idx4 = E(nc.sbuf_tensor(host.u("idx4"), [128, 2 * NCT], I32))
        idx2 = E(nc.sbuf_tensor(host.u("idx2"), [128, 2 * NCT], I32))
        idxw = E(nc.sbuf_tensor(host.u("idxw"), [128, 8 * NCT], F32))
        wcol4 = E(nc.sbuf_tensor(host.u("wcol4"), [128, 2 * NCT], F32))
        vw_sb = E(nc.sbuf_tensor(host.u("vw_sb"), [128, 8], F32))
        dumc = E(nc.sbuf_tensor(host.u("dumc"), [128, 1], F32))
        base8 = E(nc.sbuf_tensor(host.u("base8"), [1, 8], F32))
        w1_sb = [E(nc.sbuf_tensor(host.u(f"w1sb{i}"), [128, 8 * 1024], BF16)) for i in range(2)]
        w2_sb = [E(nc.sbuf_tensor(host.u(f"w2sb{i}"), [128, 8 * 1024], BF16)) for i in range(2)]
        h_bf = E(nc.sbuf_tensor(host.u("h_bf"), [128, NHB * 512], BF16))
        sq_sb = E(nc.sbuf_tensor(host.u("sq_sb"), [128, 512], F32))
        c_sb = E(nc.sbuf_tensor(host.u("c_sb"), [128, 512], F32))
        u_sb = E(nc.sbuf_tensor(host.u("u_sb"), [128, 512], F32))
        t_sb = E(nc.sbuf_tensor(host.u("t_sb"), [128, 512], F32))
        xin = [E(nc.sbuf_tensor(host.u(f"xin{i}"), [128, D], F32)) for i in range(2)]
        lnt32 = E(nc.sbuf_tensor(host.u("lnt32"), [128, D], F32))
        lnf32 = E(nc.sbuf_tensor(host.u("lnf32"), [128, D], F32))
        b1_sb = [E(nc.sbuf_tensor(host.u(f"b1sb{i}"), [1, 1024], BF16)) for i in range(2)]
        b2_sb = [E(nc.sbuf_tensor(host.u(f"b2sb{i}"), [1, D], BF16)) for i in range(2)]
        wr_sb = E(nc.sbuf_tensor(host.u("wr_sb"), [128, 8 * 8], F32))
        w_st = E(nc.sbuf_tensor(host.u("w_st"), [128, NTT * 8], F32))
        s1 = E(nc.sbuf_tensor(host.u("s1"), [128, 1], F32))
        ssq = E(nc.sbuf_tensor(host.u("ssq"), [128, 1], F32))
        mean = E(nc.sbuf_tensor(host.u("mean"), [128, 1], F32))
        m2t = E(nc.sbuf_tensor(host.u("m2t"), [128, 1], F32))
        var = E(nc.sbuf_tensor(host.u("var"), [128, 1], F32))
        lnv = E(nc.sbuf_tensor(host.u("lnv"), [128, 1], F32))
        rstd = E(nc.sbuf_tensor(host.u("rstd"), [128, 1], F32))
        rm1 = E(nc.sbuf_tensor(host.u("rm1"), [128, 1], F32))
        rm2 = E(nc.sbuf_tensor(host.u("rm2"), [128, 1], F32))
        rd = E(nc.sbuf_tensor(host.u("rd"), [128, 8], F32))
        rt = E(nc.sbuf_tensor(host.u("rt"), [128, 8], F32))
        rw = E(nc.sbuf_tensor(host.u("rw"), [128, 8], F32))
        rmask = E(nc.sbuf_tensor(host.u("rmask"), [128, 8], U32))
        rneg = E(nc.sbuf_tensor(host.u("rneg"), [128, 8], F32))
        rzero = E(nc.sbuf_tensor(host.u("rzero"), [128, 8], F32))
        rexp = E(nc.sbuf_tensor(host.u("rexp"), [128, 8], F32))
        rsum = E(nc.sbuf_tensor(host.u("rsum"), [128, 1], F32))
        rrec = E(nc.sbuf_tensor(host.u("rrec"), [128, 1], F32))
        ps_h = [E(nc.psum_tensor(host.u(f"psh{i}"), [128, 512], F32)) for i in range(3)]
        ps_o = [E(nc.psum_tensor(host.u(f"pso{i}"), [128, D], F32)) for i in range(2)]
        ps_tb = E(nc.psum_tensor(host.u("ps_tb"), [128, 1024], BF16))
        s_ld = host.sem("s_ld", serial=True)
        s_w = host.sem("s_w", serial=True)
        s_pe = host.sem("s_pe")
        s_hpe = host.sem("s_hpe")
        s_ope = host.sem("s_ope")
        sv = p.esem["vector"]
        sa = p.esem["scalar"]
        s_st = host.sem("s_st", serial=True)
        s_sp = host.sem("s_sp", serial=True)
        s_ix = host.sem("s_ix", serial=True)
        s_gq = host.sem("s_gq", serial=True)
        s_oa = [host.sem(f"s_oa{i}", serial=True) for i in range(2)]
        s_oaz = [[host.sem(f"s_oaz{st_}{i}") for i in range(2)]
                 for st_ in range(4)]
        s_spz = host.sem("s_spz")
        s_bl = [host.sem(f"s_bl{i}") for i in range(2)]
        s_bs = [host.sem(f"s_bs{i}") for i in range(2)]
        s_wf = [[host.sem(f"s_wf{m}{i}") for i in range(2)]
                for m in range(2)]
        s_ws = [host.sem(f"s_ws{i}") for i in range(4)]

        p.emit("sync", lambda e: e.dma_start(
            out=wr_sb[:].rearrange("p (c n) -> p c n", c=8),
            in_=r_d.rearrange("(c p) n -> p c n", p=128)), inc=s_w, inc_n=16)
        consts_w = s_w.n

        bias_jobs = [(b1_d[:, q * 1024:(q + 1) * 1024],
                      b1b_d[:, q * 1024:(q + 1) * 1024], None) for q in range(4)]
        bias_jobs.append((b2_d[:, :], b2b_d[:, :], 2.0))
        bias_jobs = host.filter_cast("ffn0", bias_jobs)
        bias_cv = {}
        for i, (src, dst, scale) in enumerate(bias_jobs):
            sl = i % 2
            lw = [(sv, bias_cv[i - 2])] if i >= 2 else []
            lv = p.emit("sync", lambda e, sl=sl, src=src: e.dma_start(
                out=bf32[:, sl * 1024:(sl + 1) * 1024], in_=src),
                waits=lw, inc=s_bl[sl], inc_n=16)
            cw = [(s_bl[sl], lv)]
            if i >= 2:
                cw.append((s_bs[sl], (i - 2) // 2 * 16 + 16))
            if scale is None:
                cv = p.emit("vector", lambda e, sl=sl: e.tensor_copy(
                    bfb[:, sl * 1024:(sl + 1) * 1024],
                    bf32[:, sl * 1024:(sl + 1) * 1024]), waits=cw)
            else:
                cv = p.emit("vector", lambda e, sl=sl, sc=scale: e.tensor_scalar(
                    out=bfb[:, sl * 1024:(sl + 1) * 1024],
                    in0=bf32[:, sl * 1024:(sl + 1) * 1024],
                    scalar1=sc, scalar2=None, op0=ALU.mult), waits=cw)
            bias_cv[i] = cv
            p.emit("scalar", lambda e, sl=sl, dst=dst: e.dma_start(
                out=dst, in_=bfb[:, sl * 1024:(sl + 1) * 1024]),
                waits=[(sv, cv)], inc=s_bs[sl], inc_n=16)
        host.cast_milestone("ffn0", s_bs[0])
        bmark = host.__dict__.setdefault(
            "f0_bmarks", [(s_bs[0], s_bs[0].n), (s_bs[1], s_bs[1].n)])

        p.emit("vector", lambda e: e.memset(rneg[:], -1e30))
        p.emit("vector", lambda e: e.memset(rzero[:], 0.0))
        p.emit("vector", lambda e: e.memset(onesf8[:], 1.0))
        bview = big128[:].rearrange("p (b c) -> p b c", c=4)
        p.emit("vector", lambda e: e.memset(bview[:, :, 0:1], float(T)))
        p.emit("vector", lambda e: e.memset(bview[:, :, 1:2], 0.0))
        p.emit("vector", lambda e: e.memset(bview[:, :, 2:3], float(2 * 1024)))
        p.emit("vector", lambda e: e.memset(bview[:, :, 3:4], 0.0))
        p.emit("vector", lambda e: e.memset(vw_sb[:], 0.0))
        p.emit("vector", lambda e: e.tensor_scalar(
            out=dumc[:], in0=iota128[:], scalar1=float(T), scalar2=None, op0=ALU.add),
            waits=[(s_w, consts_w)])
        for i in range(4):
            p.emit("vector", lambda e, i=i: e.memset(gt[i][:], 0.0))
        zi = p.emit("vector", lambda e: e.memset(sc_sb[0][:], 0.0))
        consts_v = sv.n
        accz_st = []
        for st_ in range(4):
            for pl in range(2):
                for tz in range((2 * 1024 + 128) // 128):
                    p.emit("scalar", lambda e, st_=st_, pl=pl, tz=tz: e.dma_start(
                        out=acc2_d[st_][pl][tz * 128:(tz + 1) * 128, :],
                        in_=sc_sb[0][:]),
                        waits=[(sv, zi)], inc=s_oaz[st_][pl], inc_n=16)
            accz_st.append((s_oaz[st_][0].n, s_oaz[st_][1].n))
        for tz in range((T + 128) // 128):
            p.emit("scalar", lambda e, tz=tz: e.dma_start(
                out=lnf0_d[tz * 128:(tz + 1) * 128, :], in_=gt[0][:]),
                waits=[(sv, consts_v)], inc=s_spz, inc_n=16)
        spz_n = s_spz.n

        wload_v = {}
        buf_last_ope = [0, 0]
        b2_last_ope = [0, 0]
        wst_done = host.__dict__.setdefault("f0_wst", {})
        first_sec = "f0_wcast" not in host.__dict__
        host.f0_wcast = True
        wf_cv = [[0, 0], [0, 0]]

        seq = [(st, e, hc) for st in range(NST) for e in range(NE) for hc in range(NHC)]

        def _jit_src(m, e_, hc_, q):
            if m == 0:
                return w1_d[e_].rearrange("(fc pp) h -> pp fc h", pp=128)[
                    :, q:q + 1, hc_ * 1024:(hc_ + 1) * 1024]
            return w2_d[e_].rearrange("(ch pp) o -> pp ch o", pp=128)[
                :, hc_ * 8 + q:hc_ * 8 + q + 1, :]

        def _jit_load(m, e_, hc_, q, waits):
            wf = (w1f, w2f)[m]
            sl = q % 2
            return p.emit("sync", lambda en, wf=wf, sl=sl, src=_jit_src(m, e_, hc_, q):
                          en.dma_start(
                              out=wf[:, sl * 1024:(sl + 1) * 1024], in_=src),
                          waits=waits, inc=s_wf[m][sl], inc_n=16)

        def emit_wload_dma(idx, buf):
            st_, e_, hc_ = seq[idx]
            jit = first_sec and st_ == 0
            if jit:
                for m in range(2):
                    for q in range(2):
                        _jit_load(m, e_, hc_, q, [(sv, wf_cv[m][q % 2])])
            else:
                waits = ([(s_ope, buf_last_ope[buf]),
                          (s_ws[0 + buf], s_ws[0 + buf].n),
                          (s_ws[2 + buf], s_ws[2 + buf].n)]
                         + list(wst_done.get((e_, hc_), ())))
                p.emit("sync", lambda en, e_=e_, hc_=hc_, buf=buf: en.dma_start(
                    out=w1_sb[buf][:], in_=w1b_d[e_, hc_]),
                    waits=waits, inc=s_w, inc_n=16)
                p.emit("sync", lambda en, e_=e_, hc_=hc_, buf=buf: en.dma_start(
                    out=w2_sb[buf][:], in_=w2b_d[e_, hc_]),
                    inc=s_w, inc_n=16)
            v = p.emit("sync", lambda en, e_=e_, hc_=hc_, buf=buf: en.dma_start(
                out=b1_sb[buf][:, :], in_=b1b_d[e_:e_ + 1, hc_ * 1024:(hc_ + 1) * 1024]),
                waits=list(bmark), inc=s_w, inc_n=16)
            if hc_ == 0:
                eb = e_ % 2
                v = p.emit("sync", lambda en, e_=e_, eb=eb: en.dma_start(
                    out=b2_sb[eb][:, :], in_=b2b_d[e_:e_ + 1, :]),
                    waits=[(s_ope, b2_last_ope[eb])], inc=s_w, inc_n=16)
            wload_v[idx] = (v, 0)

        def emit_wload_cast(idx, buf):
            st_, e_, hc_ = seq[idx]
            if not (first_sec and st_ == 0):
                return
            ldc = [[s_wf[m][0].n, s_wf[m][1].n] for m in range(2)]
            cvl = 0
            for q in range(8):
                for m, wsb in enumerate((w1_sb, w2_sb)):
                    wf = (w1f, w2f)[m]
                    sl = q % 2
                    cw = [(s_wf[m][sl], ldc[m][sl])]
                    if q == 0:
                        cw += [(s_ope, buf_last_ope[buf]),
                               (s_ws[2 * m + buf], s_ws[2 * m + buf].n)]
                    cv = p.emit("vector", lambda en, wf=wf, wsb=wsb, buf=buf, q=q,
                                sl=sl: en.tensor_copy(
                                    wsb[buf][:, q * 1024:(q + 1) * 1024],
                                    wf[:, sl * 1024:(sl + 1) * 1024]),
                                waits=cw)
                    wf_cv[m][sl] = cv
                    cvl = cv
                    if q + 2 < 8:
                        ldc[m][sl] = _jit_load(m, e_, hc_, q + 2, [(sv, cv)])
            for m, wsb in enumerate((w1_sb, w2_sb)):
                wdst = (w1b_d if m == 0 else w2b_d)
                p.emit("scalar", lambda en, wdst=wdst, e_=e_, hc_=hc_, wsb=wsb,
                       buf=buf: en.dma_start(out=wdst[e_, hc_], in_=wsb[buf][:]),
                       waits=[(sv, cvl)], inc=s_ws[2 * m + buf], inc_n=16)
            wst_done[(e_, hc_)] = ((s_ws[0 + buf], s_ws[0 + buf].n),
                                   (s_ws[2 + buf], s_ws[2 + buf].n))
            wload_v[idx] = (wload_v[idx][0], cvl)

        emit_wload_dma(0, 0)
        emit_wload_cast(0, 0)

        xin_last_read = [0, 0]
        out_last_store = [0, 0]
        hdve_use = [0, 0, 0]
        pso0_free = [0]
        oacc_use = [0, 0]
        oacc_end_prev_st = 0
        hbf_ope_prev = 0
        rtr_last_read = [0]
        mask_pe_guard = [0]
        gt_pe = [0, 0, 0, 0]
        tb_free = [0]
        glnfm_guard_hpe = [0, 0]
        idx_guard_gq = [0, 0]
        idx_guard_oa = [[0, 0], [0, 0]]
        slot_guard_v = [0, 0]
        gl_v = [0, 0]
        glnfm_bias_guard = [True]
        sc_buf_guard = [(s_oa[0], 0), (s_oa[0], 0), (s_oa[0], 0)]
        res_guard = [0, 0]
        res_guard_b = [0, 0]
        ln_tv_prev = [0]
        ln_bv_prev = [0]
        sp_done = [0]
        ix_done = [0]
        gq_of = {}

        def emit_gather(st, e_i):
            gbuf = (st * NE + e_i) % 2
            glv = p.emit("gpsimd", lambda en, e_i=e_i, gbuf=gbuf: en.dma_start(
                out=idxw[:, gbuf * 16:(gbuf + 1) * 16].rearrange("p (t c) -> p t c", c=4),
                in_=slot_d[e_i * CAP:(e_i + 1) * CAP, :].rearrange("(t p) c -> p t c", p=128)),
                waits=[(s_ix, ix_done[0]), (sv, slot_guard_v[gbuf])],
                inc=s_gq, inc_n=16)
            iw = idxw[:, gbuf * 16:(gbuf + 1) * 16].rearrange("p (t c) -> p t c", c=4)
            cvi = p.emit("vector", lambda en, gbuf=gbuf, iw=iw: en.tensor_copy(
                idx4[:, gbuf * 4:(gbuf + 1) * 4].rearrange("p (t c) -> p t c", c=1),
                iw[:, :, 0:1]),
                waits=[(s_gq, glv), (s_gq, idx_guard_gq[gbuf]),
                       (s_oa[0], idx_guard_oa[gbuf][0]),
                       (s_oa[1], idx_guard_oa[gbuf][1])])
            p.emit("vector", lambda en, gbuf=gbuf, iw=iw: en.tensor_copy(
                wcol4[:, gbuf * 4:(gbuf + 1) * 4].rearrange("p (t c) -> p t c", c=1),
                iw[:, :, 1:2]))
            cvw = p.emit("vector", lambda en, gbuf=gbuf, iw=iw: en.tensor_copy(
                idx2[:, gbuf * 4:(gbuf + 1) * 4].rearrange("p (t c) -> p t c", c=1),
                iw[:, :, 2:3]))
            slot_guard_v[gbuf] = cvw
            for t in range(NCT):
                gq_of[(st, e_i, t)] = p.emit(
                    "gpsimd", lambda en, gbuf=gbuf, t=t: en.indirect_dma_start(
                        out=gt[t][:], out_offset=None, in_=lnf0_d,
                        in_offset=IndirectOffsetOnAxis(idx4[:, gbuf * 4 + t:gbuf * 4 + t + 1], 0)),
                    waits=[(sv, cvi), (s_sp, sp_done[0]), (s_pe, gt_pe[t])],
                    inc=s_gq, inc_n=16)
            idx_guard_gq[gbuf] = s_gq.n

        def emit_transposes(st, e_i):
            gbuf = (st * NE + e_i) % 2
            first_ev = [True]
            bias_guard = glnfm_bias_guard[0] and gbuf == 0
            glnfm_bias_guard[0] = glnfm_bias_guard[0] and gbuf != 0
            for t in range(NCT):
                tv = None
                for c8 in range(8):
                    tv = p.emit("tensor", lambda en, t=t, c8=c8: en.transpose(
                        ps_tb[:, c8 * 128:(c8 + 1) * 128],
                        gt[t][:, c8 * 128:(c8 + 1) * 128],
                        ident[:]),
                        waits=[(s_gq, gq_of[(st, e_i, t)]), (sv, tb_free[0])]
                        if c8 == 0 else (),
                        inc=s_pe if c8 == 7 else None)
                gt_pe[t] = tv
                evw = [(s_pe, tv)]
                if first_ev[0]:
                    evw.append((s_hpe, glnfm_guard_hpe[gbuf]))
                    if bias_guard:
                        evw += list(bmark)
                    first_ev[0] = False
                ev = p.emit("vector", lambda en, gbuf=gbuf, t=t: en.tensor_copy(
                    glnfm[gbuf][:].rearrange("p (c n) -> p c n", c=8)[
                        :, :, t * 128:(t + 1) * 128],
                    ps_tb[:].rearrange("p (c n) -> p c n", c=8)),
                    waits=evw)
                tb_free[0] = ev
            gl_v[gbuf] = sv.n

        accinB = [w1f[:], w2f[:]]
        for st in range(NST):
            first_pe_of_st = True
            for tt in range(NTT):
                g = st * NTT + tt
                b = tt % 2
                xv = p.emit("gpsimd", lambda e, g=g, b=b: e.dma_start(
                    out=xin[b][:], in_=x_d[g * 128:(g + 1) * 128, :]),
                    waits=[(sv, xin_last_read[b])], inc=s_ld, inc_n=16)
                sumv = p.emit("vector", lambda e, b=b: e.reduce_sum(s1[:], xin[b][:], axis=AX.X),
                            waits=[(s_ld, xv)])
                qv = p.emit("scalar", lambda e, b=b: e.activation(
                    out=lnt32[:], in_=xin[b][:], func=AF.Square, accum_out=ssq[:]),
                    waits=[(s_ld, xv), (s_pe, ln_tv_prev[0]), (sv, ln_bv_prev[0])])
                mv = p.emit("vector", lambda e: e.tensor_scalar(
                    out=mean[:], in0=s1[:], scalar1=1.0 / D, scalar2=None, op0=ALU.mult),
                    waits=[(sv, sumv)])
                p.emit("vector", lambda e: e.tensor_scalar(
                    out=var[:], in0=ssq[:], scalar1=1.0 / D, scalar2=None, op0=ALU.mult),
                    waits=[(sa, qv)])
                p.emit("vector", lambda e: e.tensor_mul(m2t[:], mean[:], mean[:]),
                       waits=[(sv, mv)])
                p.emit("vector", lambda e: e.tensor_sub(var[:], var[:], m2t[:]),
                       )
                vv = p.emit("vector", lambda e: e.tensor_scalar(
                    out=var[:], in0=var[:], scalar1=1e-5, scalar2=None, op0=ALU.add),
                    )
                lv = p.emit("scalar", lambda e: e.activation(
                    out=lnv[:], in_=var[:], func=AF.Ln),
                    waits=[(sv, vv)])
                nh2 = p.emit("vector", lambda e: e.tensor_scalar(
                    out=lnv[:], in0=lnv[:], scalar1=-0.5, scalar2=None, op0=ALU.mult),
                    waits=[(sa, lv)])
                rv = p.emit("scalar", lambda e: e.activation(
                    out=rstd[:], in_=lnv[:], func=AF.Exp),
                    waits=[(sv, nh2)])
                l32 = p.emit("vector", lambda e, b=b: e.tensor_scalar(
                    out=lnt32[:], in0=xin[b][:], scalar1=mean[:], scalar2=rstd[:],
                    op0=ALU.subtract, op1=ALU.mult),
                    waits=[(sa, rv)])
                xin_last_read[b] = l32
                lnbv = p.emit("vector", lambda e: e.tensor_copy(lnb_sb[:], lnt32[:]),
                              waits=[(s_sp, s_sp.n)])
                ln_bv_prev[0] = lnbv
                p.emit("gpsimd", lambda e, g=g: e.dma_start(
                    out=lnf0_d[g * 128:(g + 1) * 128, :], in_=lnb_sb[:]),
                    waits=[(sv, lnbv), (s_spz, spz_n)], inc=s_sp, inc_n=16)
                tw = [(sv, l32), (s_w, consts_w), (sv, pso0_free[0]),
                      (sa, rtr_last_read[0])]
                if first_pe_of_st:
                    tw += [(sv, oacc_end_prev_st)]
                    first_pe_of_st = False
                tv2 = None
                for c in range(8):
                    tv2 = p.emit("tensor", lambda e, c=c: e.transpose(
                        ps_o[0][:, c * 128:(c + 1) * 128],
                        lnt32[:, c * 128:(c + 1) * 128], identf[:]),
                        waits=tw if c == 0 else (),
                        inc=s_pe if c == 7 else None)
                ln_tv_prev[0] = tv2
                evf = p.emit("vector", lambda e: e.tensor_copy(lnf32[:], ps_o[0][:]),
                             waits=[(s_pe, tv2)])
                pso0_free[0] = evf
                mats = []
                for c in range(8):
                    mats.append(lambda e, c=c: e.matmul(
                        ps_o[0][:, 0:8], lnf32[:, c * 128:(c + 1) * 128],
                        wr_sb[:, c * 8:(c + 1) * 8], start=(c == 0), stop=(c == 7)))
                mmv = p.mm_group(mats, waits=[(sv, evf), (sv, oacc_end_prev_st),
                                              (sa, rtr_last_read[0])],
                                 inc=s_pe)
                m1v = p.emit("vector", lambda e: e.reduce_max(
                    rm1[:], ps_o[0][:, 0:8], axis=AX.X, negate=True),
                    waits=[(s_pe, mmv)])
                ev2 = p.emit("scalar", lambda e: e.activation(
                    out=rexp[:], in_=ps_o[0][:, 0:8], func=AF.Exp, bias=rm1[:],
                    accum_out=rsum[:]),
                    waits=[(sv, m1v)])
                rtr_last_read[0] = ev2
                rcv = p.emit("vector", lambda e: e.reciprocal(rrec[:], rsum[:]),
                             waits=[(sa, ev2)])
                t1v = p.emit("vector", lambda e: e.reduce_max(rm1[:], rexp[:], axis=AX.X),
                             )
                p.emit("vector", lambda e: e.tensor_scalar(
                    out=rd[:], in0=rexp[:], scalar1=rm1[:], scalar2=None, op0=ALU.subtract),
                    waits=[(sv, t1v)])
                p.emit("vector", lambda e: e.tensor_scalar(
                    out=rmask[:], in0=rd[:], scalar1=0.0, scalar2=None, op0=ALU.is_ge),
                    )
                p.emit("vector", lambda e, tt=tt: e.tensor_copy(
                    maskA_st[:, tt * 8:(tt + 1) * 8], rzero[:]), waits=[(sv, consts_v)])
                p.emit("vector", lambda e, tt=tt: e.copy_predicated(
                    maskA_st[:, tt * 8:(tt + 1) * 8], rmask[:], onesf8[:]))
                p.emit("vector", lambda e: e.tensor_copy(rt[:], rexp[:]))
                p.emit("vector", lambda e: e.copy_predicated(rt[:], rmask[:], rneg[:]),
                       waits=[(sv, consts_v)])
                t2v = p.emit("vector", lambda e: e.reduce_max(rm2[:], rt[:], axis=AX.X),
                             )
                p.emit("vector", lambda e: e.tensor_scalar(
                    out=rd[:], in0=rexp[:], scalar1=rm2[:], scalar2=None, op0=ALU.subtract),
                    waits=[(sv, t2v)])
                p.emit("vector", lambda e: e.tensor_scalar(
                    out=rmask[:], in0=rd[:], scalar1=0.0, scalar2=None, op0=ALU.is_ge),
                    )
                p.emit("vector", lambda e, tt=tt: e.tensor_copy(
                    maskT_st[:, tt * 8:(tt + 1) * 8], rzero[:]),
                    waits=[(s_pe, mask_pe_guard[0])])
                p.emit("vector", lambda e, tt=tt: e.copy_predicated(
                    maskT_st[:, tt * 8:(tt + 1) * 8], rmask[:], onesf8[:]))
                p.emit("vector", lambda e: e.tensor_copy(rw[:], rzero[:]),
                       waits=[(sv, consts_v)])
                sel = p.emit("vector", lambda e: e.copy_predicated(rw[:], rmask[:], rexp[:]),
                             )
                p.emit("vector", lambda e, tt=tt: e.tensor_scalar(
                    out=w_st[:, tt * 8:(tt + 1) * 8], in0=rw[:], scalar1=rrec[:],
                    scalar2=0.25, op0=ALU.mult, op1=ALU.mult),
                    waits=[(sv, rcv), (sv, sel)])
            sp_done[0] = s_sp.n

            p.emit("gpsimd", lambda e: e.dma_start(
                out=slot_d[0:8 * CAP, :].rearrange("(a b) c -> a (b c)", a=128),
                in_=big128[:]),
                waits=[(sv, consts_v), (s_gq, s_gq.n)], inc=s_ix, inc_n=16)
            bz = p.emit("vector", lambda e: e.memset(base8[:], 0.0),
                        waits=[(s_pe, mask_pe_guard[0])])
            bev = bz
            ix_scat = s_ix.n
            for tt in range(NTT):
                g = st * NTT + tt
                cw = [(sv, bev)]
                if tt == 0:
                    cw.append((sv, oacc_end_prev_st))
                cum = p.mm_group([
                    lambda e: e.matmul(ps_o[1][:, 0:8], o1r[:], base8[0:1, :],
                                       start=True, stop=False),
                    lambda e, tt=tt: e.matmul(ps_o[1][:, 0:8], tri[:],
                                              maskT_st[:, tt * 8:(tt + 1) * 8],
                                              start=False, stop=True)],
                    waits=cw, inc=s_pe)
                pev = p.emit("vector", lambda e: e.tensor_copy(ptm[:], ps_o[1][:, 0:8]),
                             waits=[(s_pe, cum)])
                selmm = p.emit("tensor", lambda e: e.matmul(
                    ps_o[1][0:1, 8:16], sel127[:], ptm[:], start=True, stop=True),
                    waits=[(sv, pev)], inc=s_pe)
                bev = p.emit("vector", lambda e: e.tensor_copy(base8[:], ps_o[1][0:1, 8:16]),
                             waits=[(s_pe, selmm)])
                mask_pe_guard[0] = selmm
                p.emit("vector", lambda e: e.tensor_scalar(
                    out=offm[:], in0=ptm[:], scalar1=-1.0, scalar2=None, op0=ALU.add))
                p.emit("vector", lambda e: e.tensor_add(offm[:], offm[:], ecst[:]),
                       waits=[(s_w, consts_w)])
                p.emit("vector", lambda e: e.tensor_scalar(
                    out=ovf8[:], in0=ptm[:], scalar1=-float(CAP), scalar2=None, op0=ALU.add))
                p.emit("vector", lambda e: e.tensor_scalar(
                    out=ovf8[:], in0=ovf8[:], scalar1=0.0, scalar2=None, op0=ALU.max))
                p.emit("vector", lambda e: e.scalar_tensor_tensor(
                    out=offm[:], in0=ovf8[:], scalar=1e6, in1=offm[:],
                    op0=ALU.mult, op1=ALU.add))
                p.emit("vector", lambda e: e.tensor_scalar(
                    out=offm[:], in0=offm[:], scalar1=dumc[:], scalar2=None,
                    op0=ALU.min))
                p.emit("vector", lambda e, tt=tt: e.tensor_mul(
                    tmp8[:], offm[:], maskA_st[:, tt * 8:(tt + 1) * 8]))
                p.emit("vector", lambda e: e.reduce_sum(offc[:, 0:1], tmp8[:], axis=AX.X))
                p.emit("vector", lambda e, tt=tt: e.tensor_sub(
                    tmp8[:], maskT_st[:, tt * 8:(tt + 1) * 8],
                    maskA_st[:, tt * 8:(tt + 1) * 8]))
                p.emit("vector", lambda e: e.tensor_mul(tmp8[:], tmp8[:], offm[:]))
                p.emit("vector", lambda e: e.reduce_sum(offc[:, 1:2], tmp8[:], axis=AX.X))
                oiv = p.emit("vector", lambda e: e.tensor_copy(offi[:], offc[:]),
                             waits=[(s_ix, ix_scat)])
                p.emit("vector", lambda e, g=g: e.tensor_scalar(
                    out=viof[:], in0=iota128[:], scalar1=float(g * 128), scalar2=None,
                    op0=ALU.add))
                for sl in range(2):
                    p.emit("vector", lambda e, sl=sl: e.tensor_copy(
                        vw_sb[:, sl * 4:sl * 4 + 1], viof[:]))
                    p.emit("vector", lambda e, sl=sl, tt=tt: e.tensor_scalar(
                        out=vw_sb[:, sl * 4 + 2:sl * 4 + 3], in0=iota128[:],
                        scalar1=2.0, scalar2=float(tt * 256 + sl),
                        op0=ALU.mult, op1=ALU.add))
                p.emit("vector", lambda e, tt=tt: e.tensor_mul(
                    tmp8[:], w_st[:, tt * 8:(tt + 1) * 8],
                    maskA_st[:, tt * 8:(tt + 1) * 8]))
                p.emit("vector", lambda e: e.reduce_sum(vw_sb[:, 1:2], tmp8[:], axis=AX.X))
                p.emit("vector", lambda e, tt=tt: e.tensor_sub(
                    tmp8[:], maskT_st[:, tt * 8:(tt + 1) * 8],
                    maskA_st[:, tt * 8:(tt + 1) * 8]))
                p.emit("vector", lambda e, tt=tt: e.tensor_mul(
                    tmp8[:], tmp8[:], w_st[:, tt * 8:(tt + 1) * 8]))
                vwv = p.emit("vector", lambda e: e.reduce_sum(vw_sb[:, 5:6], tmp8[:], axis=AX.X))
                for sl in range(2):
                    p.emit("gpsimd", lambda e, sl=sl: e.indirect_dma_start(
                        out=slot_d,
                        out_offset=IndirectOffsetOnAxis(offi[:, sl:sl + 1], 0),
                        in_=vw_sb[:, sl * 4:sl * 4 + 4], in_offset=None),
                        waits=[(sv, vwv), (sv, oiv)], inc=s_ix, inc_n=16)
                ix_scat = s_ix.n
            ix_done[0] = s_ix.n

            emit_gather(st, 0)
            for e_i in range(NE):
                gbuf = (st * NE + e_i) % 2
                emit_transposes(st, e_i)
                if e_i + 1 < NE:
                    emit_gather(st, e_i + 1)
                for hc in range(NHC):
                    k = st * NE * NHC + e_i * NHC + hc
                    buf = k % 2
                    wv_b, wv_v = wload_v[k]
                    if k + 1 < len(seq):
                        emit_wload_dma(k + 1, 1 - buf)
                    hdve_vals = {}
                    for hb in range(NHB):
                        pp = hb % 3
                        waits = [(s_w, wv_b), (sv, wv_v), (sv, hdve_use[pp]),
                                 (sv, gl_v[gbuf]), (s_ope, hbf_ope_prev)]
                        mats = []
                        for kk in range(8):
                            mats.append(lambda e, pp=pp, bu=buf, kk=kk, hb=hb, gbuf=gbuf: e.matmul(
                                ps_h[pp][:],
                                w1_sb[bu][:, kk * 1024 + hb * 128: kk * 1024 + (hb + 1) * 128],
                                glnfm[gbuf][:, kk * 512:(kk + 1) * 512],
                                start=(kk == 0), stop=False))
                        mats.append(lambda e, pp=pp, bu=buf, hb=hb: e.matmul(
                            ps_h[pp][:], b1_sb[bu][0:1, hb * 128:(hb + 1) * 128],
                            ones512[:], start=False, stop=True))
                        hpe = p.mm_group(mats, waits=waits, inc=s_hpe)
                        sqv = p.emit("scalar", lambda e, pp=pp: e.activation(
                            out=sq_sb[:], in_=ps_h[pp][:], func=AF.Square),
                            waits=[(s_hpe, hpe)])
                        p.emit("vector", lambda e: e.tensor_scalar(
                            out=c_sb[:], in0=sq_sb[:], scalar1=S_GELU * C_GELU,
                            scalar2=S_GELU, op0=ALU.mult, op1=ALU.add),
                            waits=[(sa, sqv)])
                        uv = p.emit("vector", lambda e, pp=pp: e.tensor_tensor(
                            u_sb[:], c_sb[:], ps_h[pp][:], op=ALU.mult),
                            )
                        tnv = p.emit("scalar", lambda e: e.activation(
                            out=t_sb[:], in_=u_sb[:], func=AF.Tanh),
                            waits=[(sv, uv)])
                        hdve_use[pp] = p.emit("vector",
                            lambda e, pp=pp, hb=hb: e.scalar_tensor_tensor(
                                out=h_bf[:, hb * 512:(hb + 1) * 512], in0=t_sb[:],
                                scalar=1.0, in1=ps_h[pp][:], op0=ALU.add, op1=ALU.mult),
                            waits=[(sa, tnv)], )
                        hdve_vals[hb] = hdve_use[pp]
                    glnfm_guard_hpe[gbuf] = s_hpe.n
                    if k + 1 < len(seq):
                        emit_wload_cast(k + 1, 1 - buf)
                    for t in range(NCT):
                        po = t % 2
                        first = True
                        ope = None
                        if hc == 0:
                            for nh in range(2):
                                p.emit("tensor", lambda e, po=po, nh=nh, eb=e_i % 2: e.matmul(
                                    ps_o[po][:, nh * 512:(nh + 1) * 512],
                                    ones128[:], b2_sb[eb][0:1, nh * 512:(nh + 1) * 512],
                                    start=True, stop=False),
                                    waits=[(sv, oacc_use[po])] if nh == 0 else ())
                            first = False
                        for hb in range(NHB):
                            for nh in range(2):
                                w2x = [(sv, hdve_vals[hb])]
                                if first:
                                    w2x.append((sv, oacc_use[po]))
                                    first = False
                                last = (nh == 1 and hb == NHB - 1)
                                ope = p.emit("tensor", lambda e, po=po, bu=buf, hb=hb,
                                             nh=nh, t=t, hc=hc: e.matmul(
                                    ps_o[po][:, nh * 512:(nh + 1) * 512],
                                    h_bf[:, hb * 512 + t * 128: hb * 512 + (t + 1) * 128],
                                    w2_sb[bu][:, hb * 1024 + nh * 512: hb * 1024 + (nh + 1) * 512],
                                    start=(hb == 0 and hc != 0), stop=(hb == NHB - 1)),
                                    waits=w2x, inc=s_ope if last else None)
                        if hc == 0:
                            oacc_use[po] = p.emit("vector",
                                lambda e, po=po, t=t: e.tensor_copy(
                                    eacc[:, t * D:(t + 1) * D], ps_o[po][:]),
                                waits=[(s_ope, ope)])
                        else:
                            oacc_use[po] = p.emit("vector",
                                lambda e, po=po, t=t: e.tensor_add(
                                    eacc[:, t * D:(t + 1) * D], ps_o[po][:],
                                    eacc[:, t * D:(t + 1) * D]),
                                waits=[(s_ope, ope)])
                        if hc == NHC - 1:
                            bi = (e_i * NCT + t) % 3
                            cs = t % 2
                            scv = p.emit("vector", lambda e, bi=bi, t=t, gbuf=gbuf:
                                         e.tensor_scalar(
                                out=sc_sb[bi][:], in0=eacc[:, t * D:(t + 1) * D],
                                scalar1=wcol4[:, gbuf * 4 + t:gbuf * 4 + t + 1],
                                scalar2=None, op0=ALU.mult),
                                waits=[sc_buf_guard[bi]]
                                + [(s_oaz[s_][p_], accz_st[s_][p_])
                                   for s_ in range(4) for p_ in range(2)]
                                + ([(s_st, out_last_store[bi])] if bi < 2 else []))
                            p.emit("gpsimd", lambda e, bi=bi, cs=cs, t=t, gbuf=gbuf,
                                   st=st: e.indirect_dma_start(
                                out=acc2_d[st][cs],
                                out_offset=IndirectOffsetOnAxis(idx2[:, gbuf * 4 + t:gbuf * 4 + t + 1], 0),
                                in_=sc_sb[bi][:], in_offset=None, compute_op=ALU.add),
                                waits=[(sv, scv)], inc=s_oa[cs], inc_n=16)
                            sc_buf_guard[bi] = (s_oa[cs], s_oa[cs].n)
                    hbf_ope_prev = s_ope.n
                    if hc == 0:
                        b2_last_ope[e_i % 2] = s_ope.n
                    buf_last_ope[buf] = s_ope.n
                idx_guard_oa[gbuf] = [s_oa[0].n, s_oa[1].n]
            oacc_end_prev_st = sv.n

            oa_final = [s_oa[0].n, s_oa[1].n]
            for tt in range(st * NTT, (st + 1) * NTT):
                b = tt % 2
                xv = p.emit("gpsimd", lambda e, tt=tt, b=b: e.dma_start(
                    out=xin[b][:], in_=x_d[tt * 128:(tt + 1) * 128, :]),
                    waits=[(sv, xin_last_read[b])], inc=s_ld, inc_n=16)
                aw = [(sv, res_guard[b]),
                      (s_oa[0], oa_final[0]), (s_oa[1], oa_final[1]),
                      (s_oaz[st][0], accz_st[st][0]),
                      (s_oaz[st][1], accz_st[st][1])]
                tl = tt - st * NTT
                av = p.emit("sync", lambda e, tl=tl, b=b, st=st: e.dma_start(
                    out=accin2[b].rearrange("p (two d) -> p two d", two=2),
                    in_=acc2_d[st][0][tl * 256:(tl + 1) * 256, :].rearrange(
                        "(p two) d -> p two d", two=2)),
                    waits=aw, inc=s_w, inc_n=16)
                awb = [(sv, res_guard_b[b])]
                if st == 0 and tl < 2:
                    awb += [(sv, wf_cv[b][0]), (sv, wf_cv[b][1])]
                avb = p.emit("sync", lambda e, tl=tl, b=b, st=st: e.dma_start(
                    out=accinB[b].rearrange("p (two d) -> p two d", two=2),
                    in_=acc2_d[st][1][tl * 256:(tl + 1) * 256, :].rearrange(
                        "(p two) d -> p two d", two=2)),
                    waits=awb, inc=s_w, inc_n=16)
                rv2a = p.emit("vector", lambda e, b=b: e.tensor_add(
                    out_sb[b], xin[b][:], accin2[b][:, 0:D]),
                    waits=[(s_ld, xv), (s_w, av), (s_st, out_last_store[b]),
                           sc_buf_guard[b]],
                    )
                p.emit("vector", lambda e, b=b: e.tensor_add(
                    out_sb[b], out_sb[b], accin2[b][:, D:2 * D]),
                    )
                p.emit("vector", lambda e, b=b: e.tensor_add(
                    out_sb[b], out_sb[b], accinB[b][:, 0:D]),
                    waits=[(s_w, avb)])
                rv2 = p.emit("vector", lambda e, b=b: e.tensor_add(
                    out_sb[b], out_sb[b], accinB[b][:, D:2 * D]),
                    )
                xin_last_read[b] = rv2a
                res_guard[b] = rv2
                res_guard_b[b] = rv2
                out_last_store[b] = p.emit("gpsimd", lambda e, tt=tt, b=b: e.dma_start(
                    out=xo_d[tt * 128:(tt + 1) * 128, :], in_=out_sb[b]),
                    waits=[(sv, rv2)], inc=s_st, inc_n=16)

        p.wait("gpsimd", [(s_st, s_st.n)])
    if own:
        return host.finish()
    return None


def build_attn0(debug=False, host=None):
    own = host is None
    if own:
        host = Host(standalone=True)
    nc = host.nc
    host.prefix = "a0_"
    x_d = host.x_in()
    wq_d = host.inp("wq", [D, D], F32)
    bq_d = host.inp("bq", [D], F32)
    wk_d = host.inp("wk", [D, D], F32)
    bk_d = host.inp("bk", [D], F32)
    wv_d = host.inp("wv", [D, D], F32)
    bv_d = host.inp("bv", [D], F32)
    wl_d = host.inp("wl", [D, D], F32)
    bl_d = host.inp("bl", [D], F32)
    wo_d = host.inp("wo", [D, D], F32)
    bo_d = host.inp("bo", [D], F32)
    xo_d = host.x_out()
    wqb_d = host.scratch("a0wqb", [D, D], BF16)
    wkb_d = host.scratch("a0wkb", [D, D], BF16)
    wvb_d = host.scratch("a0wvb", [D, D], BF16)
    wlb_d = host.scratch("a0wlb", [D, D], BF16)
    wob_d = host.scratch("a0wob", [D, D], BF16)
    oacc_d = host.scratch("a0oaccs", [T, D], F32)
    if debug:
        dq_d = nc.dram_tensor("dq", [128, 4 * T], F32, kind="ExternalOutput").ap()
        dk_d = nc.dram_tensor("dk", [128, 4 * T], F32, kind="ExternalOutput").ap()
        dv_d = nc.dram_tensor("dv", [128, 32 * 520], F32, kind="ExternalOutput").ap()
        dU_d = nc.dram_tensor("dU", [128, 4 * 512], F32, kind="ExternalOutput").ap()
        drc_d = nc.dram_tensor("drc", [128, 32], F32, kind="ExternalOutput").ap()
        dnc_d = nc.dram_tensor("dnc", [128, 8], F32, kind="ExternalOutput").ap()
        do_d = nc.dram_tensor("do", [128, D], F32, kind="ExternalOutput").ap()
        dp_d = nc.dram_tensor("dp", [128, D], F32, kind="ExternalOutput").ap()
    identf = host.const("identf")
    ones128 = host.const("ones128")
    ones64 = host.const("ones64")

    NG, NH, HD = 2, 8, 64
    NTG = 8
    NQG = 8
    NKC = 32

    with host.section("attn0") as (ctx, p):
        E = ctx.enter_context
        sv = p.esem["vector"]
        sa = p.esem["scalar"]
        q_sb = E(nc.sbuf_tensor(host.u("q_sb"), [128, 4 * T], BF16))
        k_sb = E(nc.sbuf_tensor(host.u("k_sb"), [128, 4 * T], BF16))
        v_sb = E(nc.sbuf_tensor(host.u("v_sb"), [128, 32 * 520], BF16))
        lnA = E(nc.sbuf_tensor(host.u("lnA"), [128, 8 * 512], BF16))
        tA = E(nc.sbuf_tensor(host.u("tA"), [128, 8 * 512], BF16))
        wq_sb = E(nc.sbuf_tensor(host.u("wq_sb"), [128, 8 * 512], BF16))
        wk_sb = E(nc.sbuf_tensor(host.u("wk_sb"), [128, 8 * 1024], BF16))
        wv_sb = E(nc.sbuf_tensor(host.u("wv_sb"), [128, 8 * 1024], BF16))
        wl_sb = E(nc.sbuf_tensor(host.u("wl_sb"), [128, 8 * 1024], BF16))
        wo_sb = E(nc.sbuf_tensor(host.u("wo_sb"), [128, 4 * 1024], BF16))
        U8 = E(nc.sbuf_tensor(host.u("U8"), [128, 4 * 512], BF16))
        out32 = [E(nc.sbuf_tensor(host.u(f"out32{i}"), [128, D], F32)) for i in range(2)]
        xin = [E(nc.sbuf_tensor(host.u(f"xin{i}"), [128, D], F32)) for i in range(2)]
        e_bf = [tA[:, 0:512], tA[:, 512:1024]]
        oin = [lnA[:, 0:2048].bitcast(F32), lnA[:, 2048:4096].bitcast(F32)]
        bo_bc = E(nc.sbuf_tensor(host.u("bo_bc"), [128, D], BF16))
        bqcol = E(nc.sbuf_tensor(host.u("bqcol"), [128, 8], F32))
        blkcol = E(nc.sbuf_tensor(host.u("blkcol"), [128, 8], F32))
        bkcol_bf = E(nc.sbuf_tensor(host.u("bkcol_bf"), [128, 8], BF16))
        bvcol_bf = E(nc.sbuf_tensor(host.u("bvcol_bf"), [128, 8], BF16))
        bcol32 = E(nc.sbuf_tensor(host.u("bcol32"), [128, 8], F32))
        blrow = v_sb[:, 0:2048].bitcast(F32)[0:1, 0:1024]
        brow = v_sb[:, 2048:4096].bitcast(F32)[0:1, 0:1024]
        blvrow_bf = E(nc.sbuf_tensor(host.u("blvrow_bf"), [1, D], BF16))
        borow_bf = E(nc.sbuf_tensor(host.u("borow_bf"), [1, D], BF16))
        nc8 = E(nc.sbuf_tensor(host.u("nc8"), [128, 8], F32))
        rc_sb = E(nc.sbuf_tensor(host.u("rc_sb"), [128, 32], F32))
        rrow = tA[:, 1024:2048].bitcast(F32)[0:1, 0:512]
        rrow2 = tA[:, 2560:3584].bitcast(F32)[0:1, 0:512]
        sqn = tA[0:64, 2048:2560]
        mq2 = E(nc.sbuf_tensor(host.u("mq2"), [1, 1], F32))
        mk2 = E(nc.sbuf_tensor(host.u("mk2"), [1, 1], F32))
        cb1 = E(nc.sbuf_tensor(host.u("cb1"), [1, 1], F32))
        cb_bf = E(nc.sbuf_tensor(host.u("cb_bf"), [1, 1], BF16))
        lnt32 = out32[1]
        sqacc = out32[0]
        s1 = E(nc.sbuf_tensor(host.u("s1"), [128, 1], F32))
        ssq = E(nc.sbuf_tensor(host.u("ssq"), [128, 1], F32))
        mean = E(nc.sbuf_tensor(host.u("mean"), [128, 1], F32))
        m2t = E(nc.sbuf_tensor(host.u("m2t"), [128, 1], F32))
        var = E(nc.sbuf_tensor(host.u("var"), [128, 1], F32))
        lnv = E(nc.sbuf_tensor(host.u("lnv"), [128, 1], F32))
        rstd = E(nc.sbuf_tensor(host.u("rstd"), [128, 1], F32))
        ps_s = [E(nc.psum_tensor(host.u(f"pss{i}"), [128, 512], F32)) for i in range(2)]
        ps_u = E(nc.psum_tensor(host.u("psu"), [65, 512], F32))
        ps_o = [E(nc.psum_tensor(host.u(f"pso{i}"), [128, D], F32)) for i in range(2)]
        ps_sm = E(nc.psum_tensor(host.u("pssm"), [128, 512], F32))
        s_w = host.sem("s_w", serial=True)
        s_ld = host.sem("s_ld", serial=True)
        s_cast = host.sem("s_cast", serial=True)
        s_st = host.sem("s_st", serial=True)
        s_oa = host.sem("s_oa", serial=True)
        s_pe = host.sem("s_pe")
        s_spe = host.sem("s_spe")
        s_upe = host.sem("s_upe")
        s_wpe = host.sem("s_wpe")

        cin = q_sb[:, 0:8192].bitcast(F32)
        cin2 = q_sb[:, 8192:16384].bitcast(F32)
        cins = [cin, cin2]
        couts = [k_sb[:, 0:4096], k_sb[:, 4096:8192]]
        cast_jobs = []
        for wsrc, wdst in ((wq_d, wqb_d), (wk_d, wkb_d), (wv_d, wvb_d),
                           (wl_d, wlb_d), (wo_d, wob_d)):
            for hq in range(2):
                cast_jobs.append((
                    wsrc.rearrange("(c pp) m -> pp c m", pp=128)[:, hq * 4:(hq + 1) * 4, :],
                    wdst.rearrange("(c pp) m -> pp c m", pp=128)[:, hq * 4:(hq + 1) * 4, :]))
        cast_jobs = host.filter_cast("attn0", cast_jobs)
        cast_v = {}
        cast_base = s_cast.n
        for i, (src, dst) in enumerate(cast_jobs):
            b = i % 2
            waits = []
            if i >= 2:
                waits.append((s_cast, cast_base + (i - 1) * 16))
                waits.append((sv, cast_v[i - 2]))
            ld_v = p.emit("sync", lambda e, s=src, b=b: e.dma_start(
                out=cins[b].rearrange("p (c n) -> p c n", c=4), in_=s),
                waits=waits, inc=s_w, inc_n=16)
            cv = p.emit("vector", lambda e, b=b: e.tensor_copy(couts[b][:], cins[b][:]),
                        waits=[(s_w, ld_v)])
            cast_v[i] = cv
            p.emit("gpsimd", lambda e, d=dst, b=b: e.dma_start(
                out=d, in_=couts[b].rearrange("p (c n) -> p c n", c=4)),
                waits=[(sv, cv)], inc=s_cast, inc_n=16)
        cast_done = host.cast_milestone("attn0", s_cast)

        p.emit("sync", lambda e: e.dma_start(
            out=bqcol[:], in_=bq_d.rearrange("(c p) -> p c", p=128),
            allow_slow_non_contiguous=True), inc=s_w, inc_n=16)
        p.emit("sync", lambda e: e.dma_start(
            out=blrow[:], in_=bl_d.rearrange("(o n) -> o n", o=1)), inc=s_w, inc_n=16)
        bk_v = p.emit("sync", lambda e: e.dma_start(
            out=bcol32[:], in_=bk_d.rearrange("(c p) -> p c", p=128),
            allow_slow_non_contiguous=True), inc=s_w, inc_n=16)
        bk2_v = p.emit("vector", lambda e: e.tensor_copy(bkcol_bf[:], bcol32[:]),
                       waits=[(s_w, bk_v)])
        bv_v = p.emit("sync", lambda e: e.dma_start(
            out=bcol32[:], in_=bv_d.rearrange("(c p) -> p c", p=128),
            allow_slow_non_contiguous=True),
            waits=[(sv, bk2_v)], inc=s_w, inc_n=16)
        bv2_v = p.emit("vector", lambda e: e.tensor_copy(bvcol_bf[:], bcol32[:]),
                       waits=[(s_w, bv_v)])
        bo_v = p.emit("sync", lambda e: e.dma_start(
            out=brow[:], in_=bo_d.rearrange("(o n) -> o n", o=1)), inc=s_w, inc_n=16)
        bo_c = p.emit("vector", lambda e: e.tensor_copy(borow_bf[:], brow[:]),
                      waits=[(s_w, bo_v)])
        consts_w = s_w.n

        wl_v = p.emit("sync", lambda e: e.dma_start(
            out=wl_sb[:].rearrange("p (c n) -> p c n", c=8),
            in_=wlb_d.rearrange("(c pp) m -> pp c m", pp=128)),
            waits=[(s_cast, cast_done)], inc=s_w, inc_n=16)

        blk_parts = []
        for nh in range(2):
            mats = []
            for c in range(8):
                mats.append(lambda e, c=c, nh=nh: e.matmul(
                    ps_sm[0:1, 0:512], bkcol_bf[:, c:c + 1],
                    wl_sb[:, c * 1024 + nh * 512: c * 1024 + (nh + 1) * 512],
                    start=(c == 0), stop=(c == 7)))
            waits = [(s_w, wl_v), (sv, bk2_v)]
            if nh == 1:
                waits.append((sv, blk_parts[0]))
            g1 = p.mm_group(mats, waits=waits, inc=s_pe)
            ev = p.emit("vector", lambda e, nh=nh: e.tensor_add(
                brow[0:1, nh * 512:(nh + 1) * 512], ps_sm[0:1, 0:512],
                blrow[0:1, nh * 512:(nh + 1) * 512]),
                waits=[(s_pe, g1), (sa, 0), (sv, bo_c)])
            blk_parts.append(ev)
        tvs = None
        for c in range(8):
            tvs = p.emit("tensor", lambda e, c=c: e.transpose(
                ps_sm[:, c:c + 1], brow[0:1, c * 128:(c + 1) * 128], identf[0:1, 0:1]),
                waits=[(sv, blk_parts[1])] if c == 0 else (),
                inc=s_pe if c == 7 else None)
        blk_done = p.emit("vector", lambda e: e.tensor_copy(blkcol[:], ps_sm[:, 0:8]),
                          waits=[(s_pe, tvs)])
        blv_parts = []
        for nh in range(2):
            mats = []
            for c in range(8):
                mats.append(lambda e, c=c, nh=nh: e.matmul(
                    ps_sm[0:1, 0:512], bvcol_bf[:, c:c + 1],
                    wl_sb[:, c * 1024 + nh * 512: c * 1024 + (nh + 1) * 512],
                    start=(c == 0), stop=(c == 7)))
            waits = [(s_w, wl_v), (sv, bv2_v), (sv, blk_done)]
            if nh == 1:
                waits.append((sv, blv_parts[0]))
            g1 = p.mm_group(mats, waits=waits, inc=s_pe)
            ev = p.emit("vector", lambda e, nh=nh: e.tensor_add(
                brow[0:1, nh * 512:(nh + 1) * 512], ps_sm[0:1, 0:512],
                blrow[0:1, nh * 512:(nh + 1) * 512]),
                waits=[(s_pe, g1)])
            blv_parts.append(ev)
        blv_done = p.emit("vector", lambda e: e.tensor_copy(blvrow_bf[:], brow[:]),
                          waits=[(sv, blv_parts[1])])
        bom = []
        for nh in range(2):
            bom.append(lambda e, nh=nh: e.matmul(
                ps_o[0][:, nh * 512:(nh + 1) * 512], ones128[:],
                borow_bf[0:1, nh * 512:(nh + 1) * 512], start=True, stop=True))
        bog = p.mm_group(bom, waits=[(sv, bo_c), (s_w, consts_w)], inc=s_pe)
        bo_done = p.emit("vector", lambda e: e.tensor_copy(bo_bc[:], ps_o[0][:]),
                         waits=[(s_pe, bog)])
        vones = p.emit("vector", lambda e: e.memset(
            v_sb[:].rearrange("p (n c) -> p n c", c=65)[:, :, 64:65], 1.0))

        xin_last_read = [0, 0]
        out32_free = [(s_st, 0), (s_st, 0)]
        oin_last = [0, 0]
        wo_end_prev_qg = 0
        pso_last_read = [bo_done, 0]
        pss_last_read = [0, 0]
        pss_last_read_sa = [0, 0]
        oa_written = {}
        g_end_sa = 0
        g_end_sv = 0
        g_end_wpe = 0
        g_end_spe = 0
        g_end_upe = 0

        for g in range(NG):
            wq_v = p.emit("sync", lambda e, g=g: e.dma_start(
                out=wq_sb[:].rearrange("p (c n) -> p c n", c=8),
                in_=wqb_d.rearrange("(c pp) m -> pp c m", pp=128)[:, :, g * 512:(g + 1) * 512]),
                waits=[(s_cast, cast_done), (s_spe, g_end_spe), (sv, g_end_sv)],
                inc=s_w, inc_n=16)
            wk_v = p.emit("sync", lambda e: e.dma_start(
                out=wk_sb[:].rearrange("p (c n) -> p c n", c=8),
                in_=wkb_d.rearrange("(c pp) m -> pp c m", pp=128)), inc=s_w, inc_n=16)
            wv_v = p.emit("sync", lambda e: e.dma_start(
                out=wv_sb[:].rearrange("p (c n) -> p c n", c=8),
                in_=wvb_d.rearrange("(c pp) m -> pp c m", pp=128)), inc=s_w, inc_n=16)
            wo_v = p.emit("sync", lambda e, g=g: e.dma_start(
                out=wo_sb[:].rearrange("p (c n) -> p c n", c=4),
                in_=wob_d.rearrange("(c pp) m -> pp c m", pp=128)[:, g * 4:(g + 1) * 4, :]),
                waits=[(s_wpe, g_end_wpe)], inc=s_w, inc_n=16)
            wload = max(wq_v, wk_v, wv_v, wl_v)

            first_pe = True
            first_ld = True
            for tg in range(NTG):
                for t4 in range(4):
                    gt = tg * 4 + t4
                    b = gt % 2
                    ldw = [(sv, xin_last_read[b])]
                    if first_ld:
                        ldw += [(s_oa, s_oa.n), (s_st, s_st.n)]
                        first_ld = False
                    xv = p.emit("gpsimd", lambda e, gt=gt, b=b: e.dma_start(
                        out=xin[b][:], in_=x_d[gt * 128:(gt + 1) * 128, :]),
                        waits=ldw, inc=s_ld, inc_n=16)
                    sumv = p.emit("vector", lambda e, b=b: e.reduce_sum(
                        s1[:], xin[b][:], axis=AX.X), waits=[(s_ld, xv)])
                    qv = p.emit("scalar", lambda e, b=b: e.activation(
                        out=sqacc[:], in_=xin[b][:], func=AF.Square, accum_out=ssq[:]),
                        waits=[(s_ld, xv), (sv, g_end_sv if first_pe else 0)])
                    p.emit("vector", lambda e: e.tensor_scalar(
                        out=mean[:], in0=s1[:], scalar1=1.0 / D, scalar2=None,
                        op0=ALU.mult), waits=[(sv, sumv)])
                    p.emit("vector", lambda e: e.tensor_scalar(
                        out=var[:], in0=ssq[:], scalar1=1.0 / D, scalar2=None,
                        op0=ALU.mult), waits=[(sa, qv)])
                    p.emit("vector", lambda e: e.tensor_mul(m2t[:], mean[:], mean[:]))
                    p.emit("vector", lambda e: e.tensor_sub(var[:], var[:], m2t[:]))
                    vv = p.emit("vector", lambda e: e.tensor_scalar(
                        out=var[:], in0=var[:], scalar1=1e-5, scalar2=None, op0=ALU.add))
                    lv = p.emit("scalar", lambda e: e.activation(
                        out=lnv[:], in_=var[:], func=AF.Ln), waits=[(sv, vv)])
                    nh2 = p.emit("vector", lambda e: e.tensor_scalar(
                        out=lnv[:], in0=lnv[:], scalar1=-0.5, scalar2=None, op0=ALU.mult),
                        waits=[(sa, lv)])
                    rv = p.emit("scalar", lambda e: e.activation(
                        out=rstd[:], in_=lnv[:], func=AF.Exp), waits=[(sv, nh2)])
                    l32 = p.emit("vector", lambda e, b=b: e.tensor_scalar(
                        out=lnt32[:], in0=xin[b][:], scalar1=mean[:], scalar2=rstd[:],
                        op0=ALU.subtract, op1=ALU.mult), waits=[(sa, rv)])
                    xin_last_read[b] = l32
                    tw = [(sv, l32), (s_w, consts_w)]
                    if first_pe:
                        tw += [(sa, g_end_sa), (s_wpe, g_end_wpe)]
                        first_pe = False
                    tw.append((sv, pso_last_read[0]))
                    tv = None
                    for c in range(8):
                        tv = p.emit("tensor", lambda e, c=c: e.transpose(
                            ps_o[0][:, c * 128:(c + 1) * 128],
                            lnt32[:, c * 128:(c + 1) * 128], identf[:]),
                            waits=tw if c == 0 else (), inc=s_pe if c == 7 else None)
                    ev = p.emit("vector", lambda e, t4=t4: e.tensor_copy(
                        lnA[:].rearrange("p (c n) -> p c n", c=8)[:, :, t4 * 128:(t4 + 1) * 128],
                        ps_o[0][:].rearrange("p (c n) -> p c n", c=8)),
                        waits=[(s_pe, tv)])
                    pso_last_read[0] = ev
                ln_done = sv.n
                for mc in range(4):
                    pp = mc % 2
                    mats = []
                    for c in range(8):
                        mats.append(lambda e, mc=mc, c=c: e.matmul(
                            ps_s[mc % 2][:],
                            wq_sb[:, c * 512 + mc * 128: c * 512 + (mc + 1) * 128],
                            lnA[:, c * 512:(c + 1) * 512],
                            start=(c == 0), stop=(c == 7)))
                    gv = p.mm_group(mats, waits=[(sv, ln_done), (s_w, wload),
                                                 (sv, pss_last_read[pp])], inc=s_pe)
                    evq = p.emit("vector", lambda e, mc=mc, tg=tg, g=g, pp=pp: e.tensor_scalar(
                        out=q_sb[:, mc * T + tg * 512: mc * T + tg * 512 + 512],
                        in0=ps_s[pp][:], scalar1=bqcol[:, g * 4 + mc:g * 4 + mc + 1],
                        scalar2=None, op0=ALU.add),
                        waits=[(s_pe, gv), (s_w, consts_w)])
                    pss_last_read[pp] = evq
                for mc in range(8):
                    pp = mc % 2
                    mats = []
                    for c in range(8):
                        mats.append(lambda e, mc=mc, c=c: e.matmul(
                            ps_s[mc % 2][:],
                            wk_sb[:, c * 1024 + mc * 128: c * 1024 + (mc + 1) * 128],
                            lnA[:, c * 512:(c + 1) * 512],
                            start=(c == 0), stop=(c == 7)))
                    gv = p.mm_group(mats, waits=[(sv, pss_last_read[pp])], inc=s_pe)
                    evt = p.emit("vector", lambda e, mc=mc, pp=pp: e.tensor_copy(
                        tA[:, mc * 512:(mc + 1) * 512], ps_s[pp][:]),
                        waits=[(s_pe, gv)])
                    pss_last_read[pp] = evt
                t1_done = sv.n
                for mc in range(4):
                    pp = mc % 2
                    mats = []
                    for c in range(8):
                        mats.append(lambda e, mc=mc, c=c, g=g: e.matmul(
                            ps_s[mc % 2][:],
                            wl_sb[:, c * 1024 + g * 512 + mc * 128: c * 1024 + g * 512 + (mc + 1) * 128],
                            tA[:, c * 512:(c + 1) * 512],
                            start=(c == 0), stop=(c == 7)))
                    gv = p.mm_group(mats, waits=[(sv, t1_done), (sv, pss_last_read[pp])],
                                    inc=s_pe)
                    evk = p.emit("vector", lambda e, mc=mc, tg=tg, g=g, pp=pp: e.tensor_scalar(
                        out=k_sb[:, mc * T + tg * 512: mc * T + tg * 512 + 512],
                        in0=ps_s[pp][:], scalar1=blkcol[:, g * 4 + mc:g * 4 + mc + 1],
                        scalar2=0.125, op0=ALU.add, op1=ALU.mult),
                        waits=[(s_pe, gv), (sv, blk_done)])
                    pss_last_read[pp] = evk
                k_mm_done = s_pe.n
                for mc in range(8):
                    pp = mc % 2
                    mats = []
                    for c in range(8):
                        mats.append(lambda e, mc=mc, c=c: e.matmul(
                            ps_s[mc % 2][:],
                            wv_sb[:, c * 1024 + mc * 128: c * 1024 + (mc + 1) * 128],
                            lnA[:, c * 512:(c + 1) * 512],
                            start=(c == 0), stop=(c == 7)))
                    gv = p.mm_group(mats, waits=[(sv, pss_last_read[pp])], inc=s_pe)
                    evt = p.emit("vector", lambda e, mc=mc, pp=pp: e.tensor_copy(
                        tA[:, mc * 512:(mc + 1) * 512], ps_s[pp][:]),
                        waits=[(s_pe, gv), (s_pe, k_mm_done)])
                    pss_last_read[pp] = evt
                t2_done = sv.n
                for t4 in range(4):
                    pp = t4 % 2
                    mats = [lambda e, g=g, t4=t4: e.matmul(
                        ps_s[t4 % 2][:], ones128[:],
                        blvrow_bf[0:1, g * 512:(g + 1) * 512], start=True, stop=False)]
                    for c in range(8):
                        mats.append(lambda e, t4=t4, c=c, g=g: e.matmul(
                            ps_s[t4 % 2][:],
                            tA[:, c * 512 + t4 * 128: c * 512 + (t4 + 1) * 128],
                            wl_sb[:, c * 1024 + g * 512: c * 1024 + (g + 1) * 512],
                            start=False, stop=(c == 7)))
                    gv = p.mm_group(mats, waits=[(sv, t2_done), (sv, pss_last_read[pp]),
                                                 (sv, blv_done), (sv, vones)], inc=s_pe)
                    evv = p.emit("vector", lambda e, tg=tg, t4=t4, pp=pp: e.tensor_copy(
                        v_sb[:, (tg * 4 + t4) * 520:(tg * 4 + t4) * 520 + 520]
                        .rearrange("p (n c) -> p n c", c=65)[:, :, 0:64],
                        ps_s[pp][:].rearrange("p (n c) -> p n c", c=64)),
                        waits=[(s_pe, gv)])
                    pss_last_read[pp] = evv
            proj_done = sv.n
            proj_pe_done = s_pe.n
            oin_last = [proj_done, proj_done]
            if debug and g == 0:
                for name_d, src, ncol in ((dq_d, q_sb, 4 * T), (dk_d, k_sb, 4 * T),
                                          (dv_d, v_sb, 32 * 520)):
                    for off in range(0, ncol, 1024):
                        wdt = min(1024, ncol - off)
                        cvv = p.emit("vector", lambda e, src=src, off=off, w=wdt: e.tensor_copy(
                            out32[0][:, 0:w], src[:, off:off + w]),
                            waits=[(sv, proj_done)])
                        p.emit("gpsimd", lambda e, d=name_d, off=off, w=wdt, cvv=cvv: e.dma_start(
                            out=d[:, off:off + w], in_=out32[0][:, 0:w]),
                            waits=[(sv, cvv)], inc=s_st, inc_n=16)
                        p.wait("vector", [(s_st, s_st.n)])

            u_free = 0
            psm_free_pe = s_pe.n
            for hh in range(NH):
                kbase = (hh % 2) * 64
                kcol = (hh // 2) * T
                if True:
                    if True:
                        last_nmg = psm_free_pe
                        for which, buf, mx in ((0, q_sb, mq2), (1, k_sb, mk2)):
                            for chn in range(8):
                                sqw = [(s_pe, last_nmg)]
                                if chn == 0 and which == 0:
                                    sqw.append((sv, proj_done))
                                sqv = p.emit("scalar", lambda e, buf=buf, chn=chn,
                                             kb=kbase, kc2=kcol: e.activation(
                                    out=sqn[:], in_=buf[kb:kb + 64, kc2 + chn * 512: kc2 + (chn + 1) * 512],
                                    func=AF.Square),
                                    waits=sqw)
                                nmg = p.emit("tensor", lambda e: e.matmul(
                                    ps_sm[0:1, 0:512], ones64[:], sqn[:],
                                    start=True, stop=True),
                                    waits=[(sa, sqv), (sv, u_free), (s_pe, psm_free_pe)],
                                    inc=s_pe)
                                last_nmg = nmg
                                if chn == 0:
                                    mv2 = p.emit("vector", lambda e, mx=mx: e.reduce_max(
                                        mx[:], ps_sm[0:1, 0:512], axis=AX.X),
                                        waits=[(s_pe, nmg)])
                                else:
                                    p.emit("vector", lambda e: e.reduce_max(
                                        cb1[:], ps_sm[0:1, 0:512], axis=AX.X),
                                        waits=[(s_pe, nmg)])
                                    mv2 = p.emit("vector", lambda e, mx=mx: e.tensor_max(
                                        mx[:], mx[:], cb1[:]))
                                psm_free_pe = s_pe.n
                                u_free = mv2
                        p.emit("vector", lambda e: e.tensor_mul(cb1[:], mq2[:], mk2[:]))
                        lv2 = p.emit("vector", lambda e: e.tensor_scalar(
                            out=cb1[:], in0=cb1[:], scalar1=1e-30, scalar2=None, op0=ALU.add))
                        l2 = p.emit("scalar", lambda e: e.activation(
                            out=cb1[:], in_=cb1[:], func=AF.Ln), waits=[(sv, lv2)])
                        h2 = p.emit("vector", lambda e: e.tensor_scalar(
                            out=cb1[:], in0=cb1[:], scalar1=0.5, scalar2=None, op0=ALU.mult),
                            waits=[(sa, l2)])
                        e2 = p.emit("scalar", lambda e: e.activation(
                            out=cb1[:], in_=cb1[:], func=AF.Exp), waits=[(sv, h2)])
                        s2v = p.emit("vector", lambda e: e.tensor_scalar(
                            out=cb1[:], in0=cb1[:], scalar1=-1.02, scalar2=-1e-3,
                            op0=ALU.mult, op1=ALU.add), waits=[(sa, e2)])
                        cbf = p.emit("vector", lambda e: e.tensor_copy(cb_bf[:], cb1[:]))
                        ncg = p.emit("tensor", lambda e: e.matmul(
                            ps_sm[:, 0:1], ones128[:], cb_bf[:],
                            start=True, stop=True),
                            waits=[(sv, cbf), (sv, u_free), (s_pe, psm_free_pe)], inc=s_pe)
                        ncv = p.emit("vector", lambda e, hh=hh: e.tensor_copy(
                            nc8[:, hh:hh + 1], ps_sm[:, 0:1]), waits=[(s_pe, ncg)])
                        u_free = ncv
                        psm_free_pe = s_pe.n
            pending_rc = None
            for qg in range(NQG):
                for hh in range(NH):
                    kbase = (hh % 2) * 64
                    kcol = (hh // 2) * T
                    exp_v = {}
                    s_v = {}
                    u_v = {}

                    def emit_S(kc, extra=()):
                        w = list(extra)
                        if kc >= 2:
                            w.append((sa, exp_v[kc - 2]))
                        s_v[kc] = p.emit("tensor", lambda e, kc=kc, kb=kbase, kc2=kcol, qg=qg: e.matmul(
                            ps_s[kc % 2][:],
                            k_sb[kb:kb + 64, kc2 + kc * 128: kc2 + (kc + 1) * 128],
                            q_sb[kb:kb + 64, kc2 + qg * 512: kc2 + (qg + 1) * 512],
                            start=True, stop=True), waits=w, inc=s_spe)

                    emit_S(0, extra=[(sv, proj_done), (sv, pss_last_read[0]),
                                     (s_pe, proj_pe_done)])
                    emit_S(1, extra=[(sv, pss_last_read[1])])
                    for kc in range(NKC):
                        exp_v[kc] = p.emit("scalar", lambda e, kc=kc, hh=hh: e.activation(
                            out=e_bf[kc % 2][:], in_=ps_s[kc % 2][:], func=AF.Exp,
                            bias=nc8[:, hh:hh + 1]),
                            waits=[(s_spe, s_v[kc]), (sv, u_free if kc == 0 else 0),
                                   (s_upe, u_v[kc - 2] if kc >= 2 else 0)])
                        if kc + 2 < NKC:
                            emit_S(kc + 2)
                        uw = [(sa, exp_v[kc])]
                        if kc == 0:
                            uw.append((sv, u_free))
                        u_v[kc] = p.emit("tensor", lambda e, kc=kc, hh=hh: e.matmul(
                            ps_u[:], v_sb[:, kc * 520 + hh * 65: kc * 520 + hh * 65 + 65],
                            e_bf[kc % 2][:], start=(kc == 0), stop=(kc == NKC - 1)),
                            waits=uw, inc=s_upe)
                    uev = p.emit("vector", lambda e, hh=hh: e.tensor_copy(
                        U8[(hh % 2) * 64:(hh % 2) * 64 + 64,
                           (hh // 2) * 512:(hh // 2) * 512 + 512], ps_u[0:64, :]),
                        waits=[(s_upe, u_v[NKC - 1]), (s_wpe, wo_end_prev_qg)])
                    rbuf = rrow if hh % 2 == 0 else rrow2
                    rcv = p.emit("vector", lambda e, rbuf=rbuf: e.reciprocal(
                        rbuf[:], ps_u[64:65, :]),
                        waits=[(s_pe, pending_rc[3] if pending_rc else 0)])
                    u_free = rcv

                    def flush_rc(pend):
                        nonlocal psm_free_pe
                        phh, prbuf, prcv, _ = pend
                        rcev = None
                        for qt in range(4):
                            tvr = p.emit("tensor", lambda e, qt=qt, prbuf=prbuf: e.transpose(
                                ps_sm[:, qt:qt + 1], prbuf[0:1, qt * 128:(qt + 1) * 128],
                                identf[0:1, 0:1]),
                                waits=[(sv, prcv), (s_pe, psm_free_pe)] if qt == 0 else
                                      [(sv, rcev)],
                                inc=s_pe)
                            rcev = p.emit("vector", lambda e, phh=phh, qt=qt: e.tensor_copy(
                                rc_sb[:, phh * 4 + qt:phh * 4 + qt + 1], ps_sm[:, qt:qt + 1]),
                                waits=[(s_pe, s_pe.n)])
                        psm_free_pe = s_pe.n
                        return rcev

                    if pending_rc is not None:
                        flush_rc(pending_rc)
                    pending_rc = (hh, rbuf, rcv, s_pe.n)
                if pending_rc is not None:
                    flush_rc(pending_rc)
                    pending_rc = None
                heads_done = sv.n
                if debug and g == 0 and qg == 0:
                    cvv = p.emit("vector", lambda e: e.tensor_copy(
                        out32[0][:, 0:32], rc_sb[:]), waits=[(sv, heads_done)])
                    p.emit("gpsimd", lambda e, cvv=cvv: e.dma_start(
                        out=drc_d[:], in_=out32[0][:, 0:32]),
                        waits=[(sv, cvv)], inc=s_st, inc_n=16)
                    cvv = p.emit("vector", lambda e: e.tensor_copy(
                        out32[0][:, 0:8], nc8[:]), waits=[(s_st, s_st.n)])
                    p.emit("gpsimd", lambda e, cvv=cvv: e.dma_start(
                        out=dnc_d[:], in_=out32[0][:, 0:8]),
                        waits=[(sv, cvv)], inc=s_st, inc_n=16)
                    for off in range(0, 2048, 1024):
                        cvv = p.emit("vector", lambda e, off=off: e.tensor_copy(
                            out32[0][:], U8[:, off:off + 1024]), waits=[(s_st, s_st.n)])
                        p.emit("gpsimd", lambda e, off=off, cvv=cvv: e.dma_start(
                            out=dU_d[:, off:off + 1024], in_=out32[0][:]),
                            waits=[(sv, cvv)], inc=s_st, inc_n=16)
                    p.wait("vector", [(s_st, s_st.n)])
                for qt in range(4):
                    pw = qt % 2
                    gq = qg * 4 + qt
                    if g == 1:
                        ov = p.emit("gpsimd", lambda e, gq=gq, pw=pw: e.dma_start(
                            out=oin[pw][:], in_=oacc_d[gq * 128:(gq + 1) * 128, :]),
                            waits=[(sv, oin_last[pw]), (s_oa, oa_written[(qg, qt)])],
                            inc=s_ld, inc_n=16)
                    acc_v = 0
                    for hh in range(NH):
                        ph = hh % 2
                        mats = []
                        for nh in range(2):
                            mats.append(lambda e, hh=hh, nh=nh, qt=qt, ph=ph: e.matmul(
                                ps_o[ph][:, nh * 512:(nh + 1) * 512],
                                U8[(hh % 2) * 64:(hh % 2) * 64 + 64,
                                   (hh // 2) * 512 + qt * 128: (hh // 2) * 512 + (qt + 1) * 128],
                                wo_sb[(hh % 2) * 64:(hh % 2) * 64 + 64,
                                      (hh // 2) * 1024 + nh * 512: (hh // 2) * 1024 + (nh + 1) * 512],
                                start=True, stop=True))
                        ww = [(sv, heads_done), (s_w, wo_v), (sv, pso_last_read[ph])]
                        wg = p.mm_group(mats, waits=ww, inc=s_wpe)
                        stw = [(s_wpe, wg)]
                        if hh == 0:
                            if g == 0:
                                stw.append((sv, bo_done))
                                prev_ap = bo_bc
                            else:
                                stw.append((s_ld, ov))
                                prev_ap = oin[pw]
                        else:
                            prev_ap = out32[pw]
                        if hh == 0:
                            stw.append(out32_free[pw])
                        if debug and g == 0 and qg == 0 and qt == 0 and hh == 0:
                            dpv = p.emit("vector", lambda e, ph=ph: e.tensor_copy(
                                oin[1][:], ps_o[ph][:]), waits=[(s_wpe, wg)])
                            p.emit("gpsimd", lambda e, dpv=dpv: e.dma_start(
                                out=dp_d[:], in_=oin[1][:]),
                                waits=[(sv, dpv)], inc=s_st, inc_n=16)
                        acc_v = p.emit("vector",
                            lambda e, hh=hh, pw=pw, ph=ph, qt=qt, pa=prev_ap: e.scalar_tensor_tensor(
                                out=out32[pw][:], in0=ps_o[ph][:],
                                scalar=rc_sb[:, hh * 4 + qt:hh * 4 + qt + 1],
                                in1=pa[:], op0=ALU.mult, op1=ALU.add),
                            waits=stw)
                        pso_last_read[ph] = acc_v
                        if debug and g == 0 and qg == 0 and qt == 0 and hh == NH - 1:
                            p.emit("gpsimd", lambda e, pw=pw: e.dma_start(
                                out=do_d[:], in_=out32[pw][:]),
                                waits=[(sv, acc_v)], inc=s_st, inc_n=16)
                            p.wait("vector", [(s_st, s_st.n)])
                    if g == 0:
                        oa_written[(qg, qt)] = p.emit("gpsimd",
                            lambda e, gq=gq, pw=pw: e.dma_start(
                                out=oacc_d[gq * 128:(gq + 1) * 128, :], in_=out32[pw][:]),
                            waits=[(sv, acc_v)], inc=s_oa, inc_n=16)
                        out32_free[pw] = (s_oa, oa_written[(qg, qt)])
                    else:
                        b = pw
                        xv = p.emit("gpsimd", lambda e, gq=gq, b=b: e.dma_start(
                            out=xin[b][:], in_=x_d[gq * 128:(gq + 1) * 128, :]),
                            waits=[(sv, xin_last_read[b])], inc=s_ld, inc_n=16)
                        rv2 = p.emit("vector", lambda e, b=b, pw=pw: e.tensor_add(
                            out32[pw][:], out32[pw][:], xin[b][:]),
                            waits=[(s_ld, xv)])
                        xin_last_read[b] = rv2
                        oin_last[pw] = rv2
                        stv = p.emit("gpsimd", lambda e, gq=gq, pw=pw: e.dma_start(
                            out=xo_d[gq * 128:(gq + 1) * 128, :], in_=out32[pw][:]),
                            waits=[(sv, rv2)], inc=s_st, inc_n=16)
                        out32_free[pw] = (s_st, stv)
                wo_end_prev_qg = s_wpe.n
            g_end_sa = sa.n
            g_end_sv = sv.n
            g_end_wpe = s_wpe.n
            g_end_spe = s_spe.n
            g_end_upe = s_upe.n

        p.wait("gpsimd", [(s_st, s_st.n)])
    if own:
        return host.finish()
    return None


def build_ffn1(host=None):
    own = host is None
    if own:
        host = Host(standalone=True)
    nc = host.nc
    host.prefix = "f1_"
    NE = 16
    FE = 512
    x_d = host.x_in()
    r_d = host.inp("r", [D, NE], F32)
    gw_d = host.inp("gw", [NE, D, FE], F32)
    uw_d = host.inp("uw", [NE, D, FE], F32)
    dw_d = host.inp("dw", [NE, FE, D], F32)
    sg_d = host.inp("sg", [D, D], F32)
    su_d = host.inp("su", [D, D], F32)
    sd_d = host.inp("sd", [D, D], F32)
    xo_d = host.x_out()
    gwb_d = host.scratch("f1gwb", [NE, D, FE], BF16)
    uwb_d = host.scratch("f1uwb", [NE, D, FE], BF16)
    dwb_d = host.scratch("f1dwb", [NE, FE, D], BF16)
    identf = host.const("identf")
    ident = host.const("identbf")

    ST = 1024
    NST = T // ST
    NTT = ST // 128
    NTG = ST // 512

    with host.section("ffn1") as (ctx, p):
        E = ctx.enter_context
        sv = p.esem["vector"]
        sa = p.esem["scalar"]
        ln_fm = E(nc.sbuf_tensor(host.u("ln_fm"), [128, 8 * ST], BF16))
        acc = E(nc.sbuf_tensor(host.u("acc"), [128, NTT * D], F32))
        sg_sb = E(nc.sbuf_tensor(host.u("sg_sb"), [128, 8 * 1024], BF16))
        su_sb = E(nc.sbuf_tensor(host.u("su_sb"), [128, 8 * 1024], BF16))
        sd_sb = E(nc.sbuf_tensor(host.u("sd_sb"), [128, 8 * 1024], BF16))
        gw_sb = [E(nc.sbuf_tensor(host.u(f"gwsb{i}"), [128, 8 * 512], BF16)) for i in range(2)]
        uw_sb = [E(nc.sbuf_tensor(host.u(f"uwsb{i}"), [128, 8 * 512], BF16)) for i in range(2)]
        dw_sb = [E(nc.sbuf_tensor(host.u(f"dwsb{i}"), [128, 4 * 1024], BF16)) for i in range(2)]
        m2sh = E(nc.sbuf_tensor(host.u("m2sh"), [128, 8 * 512], BF16))
        m2e = E(nc.sbuf_tensor(host.u("m2e"), [128, 4 * 512], BF16))
        t_sb = E(nc.sbuf_tensor(host.u("t_sb"), [128, 512], F32))
        m1_sb = E(nc.sbuf_tensor(host.u("m1_sb"), [128, 512], F32))
        xin = [E(nc.sbuf_tensor(host.u(f"xin{i}"), [128, D], F32)) for i in range(2)]
        out_sb = [E(nc.sbuf_tensor(host.u(f"outsb{i}"), [128, D], F32)) for i in range(2)]
        lnt = E(nc.sbuf_tensor(host.u("lnt"), [128, D], BF16))
        lnt32 = E(nc.sbuf_tensor(host.u("lnt32"), [128, D], F32))
        lnf32 = E(nc.sbuf_tensor(host.u("lnf32"), [128, D], F32))
        wr_sb = E(nc.sbuf_tensor(host.u("wr_sb"), [128, 8 * NE], F32))
        w_st = E(nc.sbuf_tensor(host.u("w_st"), [128, NTT * NE], F32))
        s1 = E(nc.sbuf_tensor(host.u("s1"), [128, 1], F32))
        ssq = E(nc.sbuf_tensor(host.u("ssq"), [128, 1], F32))
        mean = E(nc.sbuf_tensor(host.u("mean"), [128, 1], F32))
        m2t = E(nc.sbuf_tensor(host.u("m2t"), [128, 1], F32))
        var = E(nc.sbuf_tensor(host.u("var"), [128, 1], F32))
        lnv = E(nc.sbuf_tensor(host.u("lnv"), [128, 1], F32))
        rstd = E(nc.sbuf_tensor(host.u("rstd"), [128, 1], F32))
        rm1 = E(nc.sbuf_tensor(host.u("rm1"), [128, 1], F32))
        rm2 = E(nc.sbuf_tensor(host.u("rm2"), [128, 1], F32))
        rd = E(nc.sbuf_tensor(host.u("rd"), [128, NE], F32))
        rt = E(nc.sbuf_tensor(host.u("rt"), [128, NE], F32))
        rw = E(nc.sbuf_tensor(host.u("rw"), [128, NE], F32))
        rmask = E(nc.sbuf_tensor(host.u("rmask"), [128, NE], U32))
        rneg = E(nc.sbuf_tensor(host.u("rneg"), [128, NE], F32))
        rzero = E(nc.sbuf_tensor(host.u("rzero"), [128, NE], F32))
        rexp = E(nc.sbuf_tensor(host.u("rexp"), [128, NE], F32))
        rsum = E(nc.sbuf_tensor(host.u("rsum"), [128, 1], F32))
        rrec = E(nc.sbuf_tensor(host.u("rrec"), [128, 1], F32))
        ps_g = E(nc.psum_tensor(host.u("psg"), [128, 512], F32))
        ps_u = E(nc.psum_tensor(host.u("psu"), [128, 512], F32))
        ps_o = [E(nc.psum_tensor(host.u(f"pso{i}"), [128, D], F32)) for i in range(2)]
        ps_t = E(nc.psum_tensor(host.u("pst"), [128, 1024], BF16))
        ps_r = E(nc.psum_tensor(host.u("psr"), [128, 512], F32))
        s_ld = host.sem("s_ld", serial=True)
        s_w = host.sem("s_w", serial=True)
        s_cast = host.sem("s_cast", serial=True)
        s_st = host.sem("s_st", serial=True)
        s_pe = host.sem("s_pe")
        s_gpe = host.sem("s_gpe")
        s_upe = host.sem("s_upe")
        s_ope = host.sem("s_ope")

        cin = [acc[:, 0:4096], acc[:, 4096:8192]]
        cout = [m2sh[:, 0:4096], ln_fm[:, 0:4096]]
        cast_jobs = []
        for e in range(NE):
            cast_jobs.append((gw_d[e].rearrange("(fc pp) f -> pp fc f", pp=128),
                              gwb_d[e].rearrange("(fc pp) f -> pp fc f", pp=128),
                              4096, 0.5))
            cast_jobs.append((uw_d[e].rearrange("(fc pp) f -> pp fc f", pp=128),
                              uwb_d[e].rearrange("(fc pp) f -> pp fc f", pp=128),
                              4096, None))
            cast_jobs.append((dw_d[e].rearrange("(fc pp) o -> pp fc o", pp=128),
                              dwb_d[e].rearrange("(fc pp) o -> pp fc o", pp=128),
                              4096, None))
        cast_jobs = host.filter_cast("ffn1", cast_jobs)
        cast_v = {}
        cast_base = s_cast.n
        for i, (src, dst, cols, scale) in enumerate(cast_jobs):
            b = i % 2
            nch = src.shape[1]
            waits = []
            if i >= 2:
                waits.append((s_cast, cast_base + (i - 1) * 16))
                waits.append((sv, cast_v[i - 2]))
            ld_v = p.emit("sync", lambda e, s=src, b=b, c=cols, nch=nch: e.dma_start(
                out=cin[b][:, 0:c].rearrange("p (c n) -> p c n", c=nch), in_=s),
                waits=waits, inc=s_w, inc_n=16)
            if scale is None:
                cv = p.emit("vector", lambda e, b=b, c=cols: e.tensor_copy(
                    cout[b][:, 0:c], cin[b][:, 0:c]), waits=[(s_w, ld_v)])
            else:
                cv = p.emit("vector", lambda e, b=b, c=cols, sc=scale: e.tensor_scalar(
                    out=cout[b][:, 0:c], in0=cin[b][:, 0:c], scalar1=sc,
                    scalar2=None, op0=ALU.mult), waits=[(s_w, ld_v)])
            cast_v[i] = cv
            p.emit("gpsimd", lambda e, d=dst, b=b, c=cols, nch=nch: e.dma_start(
                out=d, in_=cout[b][:, 0:c].rearrange("p (c n) -> p c n", c=nch)),
                waits=[(sv, cv)], inc=s_cast, inc_n=16)
        cast_done = host.cast_milestone("ffn1", s_cast)
        sh_jobs = [(sg_d, sg_sb, 0.5), (su_d, su_sb, None), (sd_d, sd_sb, None)]
        sh_done = 0
        for j, (src, dstsb, scale) in enumerate(sh_jobs):
            for hq in range(2):
                i = len(cast_jobs) + j * 2 + hq
                b = i % 2
                waits = [(s_cast, s_cast.n), (sv, cast_v.get(i - 2, 0))]
                ld_v = p.emit("sync", lambda e, s=src, b=b, hq=hq: e.dma_start(
                    out=cin[b][:].rearrange("p (c n) -> p c n", c=4),
                    in_=s.rearrange("(c pp) m -> pp c m", pp=128)[:, hq * 4:(hq + 1) * 4, :]),
                    waits=waits, inc=s_w, inc_n=16)
                if scale is None:
                    cv = p.emit("vector", lambda e, b=b, d2=dstsb, hq=hq: e.tensor_copy(
                        d2[:, hq * 4096:(hq + 1) * 4096], cin[b][:]), waits=[(s_w, ld_v)])
                else:
                    cv = p.emit("vector", lambda e, b=b, d2=dstsb, hq=hq, sc=scale: e.tensor_scalar(
                        out=d2[:, hq * 4096:(hq + 1) * 4096], in0=cin[b][:], scalar1=sc,
                        scalar2=None, op0=ALU.mult), waits=[(s_w, ld_v)])
                cast_v[i] = cv
        sh_done = sv.n
        p.emit("sync", lambda e: e.dma_start(
            out=wr_sb[:].rearrange("p (c n) -> p c n", c=8),
            in_=r_d.rearrange("(c p) n -> p c n", p=128)), inc=s_w, inc_n=16)
        consts_w = s_w.n
        p.emit("vector", lambda e: e.memset(rneg[:], -1e30))
        p.emit("vector", lambda e: e.memset(rzero[:], 0.0))
        consts_v = sv.n

        wload_v = {}
        buf_last_ope = [0, 0]
        seq = [(st, e) for st in range(NST) for e in range(NE)]

        def emit_weight_load(idx, buf):
            st_, e_ = seq[idx]
            waits = [(s_cast, cast_done), (s_ope, buf_last_ope[buf])]
            p.emit("sync", lambda en, e_=e_, buf=buf: en.dma_start(
                out=gw_sb[buf][:].rearrange("p (c n) -> p c n", c=8),
                in_=gwb_d[e_].rearrange("(fc pp) f -> pp fc f", pp=128)),
                waits=waits, inc=s_w, inc_n=16)
            p.emit("sync", lambda en, e_=e_, buf=buf: en.dma_start(
                out=uw_sb[buf][:].rearrange("p (c n) -> p c n", c=8),
                in_=uwb_d[e_].rearrange("(fc pp) f -> pp fc f", pp=128)),
                inc=s_w, inc_n=16)
            v = p.emit("sync", lambda en, e_=e_, buf=buf: en.dma_start(
                out=dw_sb[buf][:].rearrange("p (c n) -> p c n", c=4),
                in_=dwb_d[e_].rearrange("(fc pp) o -> pp fc o", pp=128)),
                inc=s_w, inc_n=16)
            wload_v[idx] = v
            return v

        emit_weight_load(0, 0)

        xin_last_read = [0, 0]
        out_last_store = [0, 0]
        g_last_read = 0
        u_last_read = 0
        oacc_use = [0, 0]
        m2_pe_done = 0
        m2sh_pe_done = 0
        hpe_end_prev_st = 0
        ln_tv_prev = [0]
        ln_bv_prev = [0]
        oacc_end_prev_st = 0
        rtr_last_read = [0]

        for st in range(NST):
            first_pe_of_st = True
            for tt in range(NTT):
                g = st * NTT + tt
                b = tt % 2
                xv = p.emit("gpsimd", lambda e, g=g, b=b: e.dma_start(
                    out=xin[b][:], in_=x_d[g * 128:(g + 1) * 128, :]),
                    waits=[(sv, xin_last_read[b])], inc=s_ld, inc_n=16)
                sumv = p.emit("vector", lambda e, b=b: e.reduce_sum(
                    s1[:], xin[b][:], axis=AX.X), waits=[(s_ld, xv)])
                qv = p.emit("scalar", lambda e, b=b: e.activation(
                    out=lnt32[:], in_=xin[b][:], func=AF.Square, accum_out=ssq[:]),
                    waits=[(s_ld, xv), (s_pe, ln_tv_prev[0]), (sv, ln_bv_prev[0])])
                mv = p.emit("vector", lambda e: e.tensor_scalar(
                    out=mean[:], in0=s1[:], scalar1=1.0 / D, scalar2=None, op0=ALU.mult),
                    waits=[(sv, sumv)])
                p.emit("vector", lambda e: e.tensor_scalar(
                    out=var[:], in0=ssq[:], scalar1=1.0 / D, scalar2=None, op0=ALU.mult),
                    waits=[(sa, qv)])
                p.emit("vector", lambda e: e.tensor_mul(m2t[:], mean[:], mean[:]),
                       waits=[(sv, mv)])
                p.emit("vector", lambda e: e.tensor_sub(var[:], var[:], m2t[:]))
                vv = p.emit("vector", lambda e: e.tensor_scalar(
                    out=var[:], in0=var[:], scalar1=1e-5, scalar2=None, op0=ALU.add))
                lv = p.emit("scalar", lambda e: e.activation(
                    out=lnv[:], in_=var[:], func=AF.Ln), waits=[(sv, vv)])
                nh2 = p.emit("vector", lambda e: e.tensor_scalar(
                    out=lnv[:], in0=lnv[:], scalar1=-0.5, scalar2=None, op0=ALU.mult),
                    waits=[(sa, lv)])
                rv = p.emit("scalar", lambda e: e.activation(
                    out=rstd[:], in_=lnv[:], func=AF.Exp), waits=[(sv, nh2)])
                l32 = p.emit("vector", lambda e, b=b: e.tensor_scalar(
                    out=lnt32[:], in0=xin[b][:], scalar1=mean[:], scalar2=rstd[:],
                    op0=ALU.subtract, op1=ALU.mult), waits=[(sa, rv)])
                xin_last_read[b] = l32
                lbf = p.emit("vector", lambda e: e.tensor_copy(lnt[:], lnt32[:]))
                ln_bv_prev[0] = lbf
                tw = [(sv, lbf), (s_w, consts_w)]
                if first_pe_of_st:
                    tw += [(sv, oacc_end_prev_st)]
                    first_pe_of_st = False
                tv = None
                for c in range(8):
                    tv = p.emit("tensor", lambda e, c=c: e.transpose(
                        ps_t[:, c * 128:(c + 1) * 128], lnt[:, c * 128:(c + 1) * 128],
                        ident[:]),
                        waits=tw if c == 0 else (), inc=s_pe if c == 7 else None)
                ew = [(s_pe, tv)]
                if tt == 0:
                    ew.append((s_gpe, hpe_end_prev_st))
                    if st == 0:
                        ew.append((s_cast, cast_done))
                p.emit("vector", lambda e, tt=tt: e.tensor_copy(
                    ln_fm[:].rearrange("p (c n) -> p c n", c=8)[:, :, tt * 128:(tt + 1) * 128],
                    ps_t[:].rearrange("p (c n) -> p c n", c=8)),
                    waits=ew)
                for half in range(2):
                    tv2 = None
                    for c in range(4):
                        cc = half * 4 + c
                        tv2 = p.emit("tensor", lambda e, cc=cc, c=c: e.transpose(
                            ps_r[:, c * 128:(c + 1) * 128],
                            lnt32[:, cc * 128:(cc + 1) * 128], identf[:]),
                            waits=[(sv, l32)] if c == 0 else (),
                            inc=s_pe if c == 3 else None)
                    ln_tv_prev[0] = tv2
                    p.emit("vector", lambda e, half=half: e.tensor_copy(
                        lnf32[:, half * 512:(half + 1) * 512], ps_r[:]),
                        waits=[(s_pe, tv2)])
                    if half == 0:
                        p.wait("tensor", [(sv, sv.n)])
                fv = sv.n
                mats = []
                for c in range(8):
                    mats.append(lambda e, c=c: e.matmul(
                        ps_o[0][:, 0:NE], lnf32[:, c * 128:(c + 1) * 128],
                        wr_sb[:, c * NE:(c + 1) * NE], start=(c == 0), stop=(c == 7)))
                mmv = p.mm_group(mats, waits=[(sv, fv), (sv, oacc_end_prev_st),
                                              (sa, rtr_last_read[0])], inc=s_pe)
                m1v = p.emit("vector", lambda e: e.reduce_max(
                    rm1[:], ps_o[0][:, 0:NE], axis=AX.X, negate=True),
                    waits=[(s_pe, mmv)])
                ev2 = p.emit("scalar", lambda e: e.activation(
                    out=rexp[:], in_=ps_o[0][:, 0:NE], func=AF.Exp, bias=rm1[:],
                    accum_out=rsum[:]), waits=[(sv, m1v)])
                rtr_last_read[0] = ev2
                rcv = p.emit("vector", lambda e: e.reciprocal(rrec[:], rsum[:]),
                             waits=[(sa, ev2)])
                t1v = p.emit("vector", lambda e: e.reduce_max(rm1[:], rexp[:], axis=AX.X))
                p.emit("vector", lambda e: e.tensor_scalar(
                    out=rd[:], in0=rexp[:], scalar1=rm1[:], scalar2=None, op0=ALU.subtract),
                    waits=[(sv, t1v)])
                p.emit("vector", lambda e: e.tensor_scalar(
                    out=rmask[:], in0=rd[:], scalar1=0.0, scalar2=None, op0=ALU.is_ge))
                p.emit("vector", lambda e: e.tensor_copy(rt[:], rexp[:]))
                p.emit("vector", lambda e: e.copy_predicated(rt[:], rmask[:], rneg[:]),
                       waits=[(sv, consts_v)])
                t2v = p.emit("vector", lambda e: e.reduce_max(rm2[:], rt[:], axis=AX.X))
                p.emit("vector", lambda e: e.tensor_scalar(
                    out=rd[:], in0=rexp[:], scalar1=rm2[:], scalar2=None, op0=ALU.subtract),
                    waits=[(sv, t2v)])
                p.emit("vector", lambda e: e.tensor_scalar(
                    out=rmask[:], in0=rd[:], scalar1=0.0, scalar2=None, op0=ALU.is_ge))
                p.emit("vector", lambda e: e.tensor_copy(rw[:], rzero[:]),
                       waits=[(sv, consts_v)])
                sel = p.emit("vector", lambda e: e.copy_predicated(rw[:], rmask[:], rexp[:]))
                p.emit("vector", lambda e, tt=tt: e.tensor_scalar(
                    out=w_st[:, tt * NE:(tt + 1) * NE], in0=rw[:], scalar1=rrec[:],
                    scalar2=0.5, op0=ALU.mult, op1=ALU.mult),
                    waits=[(sv, rcv), (sv, sel)])
            ln_phase_v = sv.n

            for tg in range(NTG):
                for fc in range(8):
                    mats = []
                    for c in range(8):
                        mats.append(lambda e, fc=fc, c=c, tg=tg: e.matmul(
                            ps_g[:],
                            sg_sb[:, c * 1024 + fc * 128: c * 1024 + (fc + 1) * 128],
                            ln_fm[:, c * ST + tg * 512: c * ST + tg * 512 + 512],
                            start=(c == 0), stop=(c == 7)))
                    gg = p.mm_group(mats, waits=[(sv, ln_phase_v), (sv, sh_done),
                                                 (sv, g_last_read)], inc=s_gpe)
                    mats = []
                    for c in range(8):
                        mats.append(lambda e, fc=fc, c=c, tg=tg: e.matmul(
                            ps_u[:],
                            su_sb[:, c * 1024 + fc * 128: c * 1024 + (fc + 1) * 128],
                            ln_fm[:, c * ST + tg * 512: c * ST + tg * 512 + 512],
                            start=(c == 0), stop=(c == 7)))
                    gu = p.mm_group(mats, waits=[(sv, u_last_read)], inc=s_upe)
                    tnv = p.emit("scalar", lambda e: e.activation(
                        out=t_sb[:], in_=ps_g[:], func=AF.Tanh),
                        waits=[(s_gpe, gg)])
                    m1v2 = p.emit("vector", lambda e: e.scalar_tensor_tensor(
                        out=m1_sb[:], in0=t_sb[:], scalar=1.0, in1=ps_g[:],
                        op0=ALU.add, op1=ALU.mult), waits=[(sa, tnv)])
                    g_last_read = m1v2
                    m2v = p.emit("vector", lambda e, fc=fc: e.tensor_tensor(
                        m2sh[:, fc * 512:(fc + 1) * 512], m1_sb[:], ps_u[:],
                        op=ALU.mult), waits=[(s_upe, gu), (s_ope, m2sh_pe_done)])
                    u_last_read = m2v
                m2_done = sv.n
                for tt2 in range(4):
                    po = tt2 % 2
                    gtile = tg * 4 + tt2
                    mats = []
                    for nh in range(2):
                        for fc in range(8):
                            mats.append(lambda e, fc=fc, nh=nh, tt2=tt2, po=po: e.matmul(
                                ps_o[po][:, nh * 512:(nh + 1) * 512],
                                m2sh[:, fc * 512 + tt2 * 128: fc * 512 + (tt2 + 1) * 128],
                                sd_sb[:, fc * 1024 + nh * 512: fc * 1024 + (nh + 1) * 512],
                                start=(fc == 0), stop=(fc == 7)))
                    ope = p.mm_group(mats, waits=[(sv, m2_done), (sv, oacc_use[po])],
                                     inc=s_ope)
                    oacc_use[po] = p.emit("vector",
                        lambda e, po=po, gtile=gtile: e.tensor_scalar(
                            out=acc[:, gtile * D:(gtile + 1) * D], in0=ps_o[po][:],
                            scalar1=0.5, scalar2=None, op0=ALU.mult),
                        waits=[(s_ope, ope)])
                m2sh_pe_done = s_ope.n

            for e_i in range(NE):
                k = st * NE + e_i
                buf = k % 2
                wv_cur = wload_v[k]
                if k + 1 < len(seq):
                    emit_weight_load(k + 1, 1 - buf)
                for tg in range(NTG):
                    for fc in range(4):
                        mats = []
                        for c in range(8):
                            mats.append(lambda e, fc=fc, c=c, tg=tg, bu=buf: e.matmul(
                                ps_g[:],
                                gw_sb[bu][:, c * 512 + fc * 128: c * 512 + (fc + 1) * 128],
                                ln_fm[:, c * ST + tg * 512: c * ST + tg * 512 + 512],
                                start=(c == 0), stop=(c == 7)))
                        gg = p.mm_group(mats, waits=[(s_w, wv_cur), (sv, g_last_read),
                                                     (sv, ln_phase_v)], inc=s_gpe)
                        mats = []
                        for c in range(8):
                            mats.append(lambda e, fc=fc, c=c, tg=tg, bu=buf: e.matmul(
                                ps_u[:],
                                uw_sb[bu][:, c * 512 + fc * 128: c * 512 + (fc + 1) * 128],
                                ln_fm[:, c * ST + tg * 512: c * ST + tg * 512 + 512],
                                start=(c == 0), stop=(c == 7)))
                        gu = p.mm_group(mats, waits=[(sv, u_last_read)], inc=s_upe)
                        tnv = p.emit("scalar", lambda e: e.activation(
                            out=t_sb[:], in_=ps_g[:], func=AF.Tanh),
                            waits=[(s_gpe, gg)])
                        m1v2 = p.emit("vector", lambda e: e.scalar_tensor_tensor(
                            out=m1_sb[:], in0=t_sb[:], scalar=1.0, in1=ps_g[:],
                            op0=ALU.add, op1=ALU.mult), waits=[(sa, tnv)])
                        g_last_read = m1v2
                        m2v = p.emit("vector", lambda e, fc=fc: e.tensor_tensor(
                            m2e[:, fc * 512:(fc + 1) * 512], m1_sb[:], ps_u[:],
                            op=ALU.mult), waits=[(s_upe, gu), (s_ope, m2_pe_done)])
                        u_last_read = m2v
                    m2_done = sv.n
                    for tt2 in range(4):
                        po = tt2 % 2
                        gtile = tg * 4 + tt2
                        mats = []
                        for nh in range(2):
                            for fc in range(4):
                                mats.append(lambda e, fc=fc, nh=nh, tt2=tt2, po=po, bu=buf: e.matmul(
                                    ps_o[po][:, nh * 512:(nh + 1) * 512],
                                    m2e[:, fc * 512 + tt2 * 128: fc * 512 + (tt2 + 1) * 128],
                                    dw_sb[bu][:, fc * 1024 + nh * 512: fc * 1024 + (nh + 1) * 512],
                                    start=(fc == 0), stop=(fc == 3)))
                        ope = p.mm_group(mats, waits=[(sv, m2_done), (sv, oacc_use[po])],
                                         inc=s_ope)
                        wcol = gtile * NE + e_i
                        oacc_use[po] = p.emit("vector",
                            lambda e, po=po, gtile=gtile, wcol=wcol: e.scalar_tensor_tensor(
                                out=acc[:, gtile * D:(gtile + 1) * D], in0=ps_o[po][:],
                                scalar=w_st[:, wcol:wcol + 1],
                                in1=acc[:, gtile * D:(gtile + 1) * D],
                                op0=ALU.mult, op1=ALU.add),
                            waits=[(s_ope, ope)])
                    m2_pe_done = s_ope.n
                buf_last_ope[buf] = s_ope.n
            hpe_end_prev_st = s_gpe.n
            oacc_end_prev_st = sv.n

            acc_done = sv.n
            for tt in range(NTT):
                g = st * NTT + tt
                b = tt % 2
                xv = p.emit("gpsimd", lambda e, g=g, b=b: e.dma_start(
                    out=xin[b][:], in_=x_d[g * 128:(g + 1) * 128, :]),
                    waits=[(sv, xin_last_read[b])], inc=s_ld, inc_n=16)
                rv2 = p.emit("vector", lambda e, b=b, tt=tt: e.tensor_add(
                    out_sb[b][:], xin[b][:], acc[:, tt * D:(tt + 1) * D]),
                    waits=[(s_ld, xv), (sv, acc_done), (s_st, out_last_store[b])])
                xin_last_read[b] = rv2
                out_last_store[b] = p.emit("gpsimd", lambda e, g=g, b=b: e.dma_start(
                    out=xo_d[g * 128:(g + 1) * 128, :], in_=out_sb[b][:]),
                    waits=[(sv, rv2)], inc=s_st, inc_n=16)

        p.wait("gpsimd", [(s_st, s_st.n)])
    if own:
        return host.finish()
    return None


def build_attn1(host=None):
    own = host is None
    if own:
        host = Host(standalone=True)
    nc = host.nc
    host.prefix = "a1_"
    DI = 2048
    DS = 16
    DTR = 64
    x_d = host.x_in()
    wi_d = host.inp("wi", [D, 2 * DI], F32)
    wc_d = host.inp("wc", [DI, 1, 3], F32)
    wx_d = host.inp("wx", [DI, 96], F32)
    wdt_d = host.inp("wdt", [DTR, DI], F32)
    bdt_d = host.inp("bdt", [DI], F32)
    wo_d = host.inp("wo", [DI, D], F32)
    lng_d = host.inp("lng", [DI], F32)
    lnb_d = host.inp("lnb", [DI], F32)
    xo_d = host.x_out()
    wib_d = host.scratch("a1wib", [D, 2 * DI], BF16)
    wob_d = host.scratch("a1wob", [DI, D], BF16)
    xsr_d = host.scratch("a1xsr", [DI, T], BF16)
    sz_d = host.scratch("a1szs", [T, DI], BF16)
    identf = host.const("identf")
    o16r = host.const("o16r")
    o16c = host.const("o16c")
    o128c = host.const("o128c")
    o16cb2 = host.const("o16cb")
    o1r = host.const("o1r")
    rep = np.zeros((16, DI), dtype=np.float32)
    for i in range(16):
        rep[i, i::16] = 1.0
    rep_d = nc.inline_tensor(_np_bf16(rep), name="repb_c" + host.prefix).ap()

    NTG = 8
    NTT = 32

    with host.section("attn1") as (ctx, p):
        E = ctx.enter_context
        sv = p.esem["vector"]
        sa = p.esem["scalar"]
        sg = p.esem["gpsimd"]
        wi_sb = E(nc.sbuf_tensor(host.u("wi_sb"), [128, 8 * 2048], BF16))
        xsh = E(nc.sbuf_tensor(host.u("xsh"), [128, 16 * 514], BF16))
        xs2 = E(nc.sbuf_tensor(host.u("xs2"), [128, 16 * 512], BF16))
        ln_fm = E(nc.sbuf_tensor(host.u("ln_fm"), [128, 8 * 512], BF16))
        g_bc = E(nc.sbuf_tensor(host.u("g_bc"), [128, DI], BF16))
        b_bc = E(nc.sbuf_tensor(host.u("b_bc"), [128, DI], BF16))
        wx_sb = E(nc.sbuf_tensor(host.u("wx_sb"), [128, 16 * 96], BF16))
        wdt_sb = E(nc.sbuf_tensor(host.u("wdt_sb"), [64, DI], BF16))
        wc_sb = E(nc.sbuf_tensor(host.u("wc_sb"), [128, 16 * 3], F32))
        bdt_sb = E(nc.sbuf_tensor(host.u("bdt_sb"), [128, 16], F32))
        xc_sb = E(nc.sbuf_tensor(host.u("xc_sb"), [128, 512], F32))
        t_sb = E(nc.sbuf_tensor(host.u("t_sb"), [128, 512], F32))
        e1_sb = E(nc.sbuf_tensor(host.u("e1_sb"), [128, 512], F32))
        sp_bf = E(nc.sbuf_tensor(host.u("sp_bf"), [128, 512], BF16))
        xp_sb = E(nc.sbuf_tensor(host.u("xp_sb"), [64, 512], BF16))
        bp_sb = E(nc.sbuf_tensor(host.u("bp_sb"), [16, 512], F32))
        dec_fm = E(nc.sbuf_tensor(host.u("dec_fm"), [16, T], F32))
        bx_fm = E(nc.sbuf_tensor(host.u("bx_fm"), [16, T], F32))
        h_fm = E(nc.sbuf_tensor(host.u("h_fm"), [16, T], F32))
        yb_fm = E(nc.sbuf_tensor(host.u("yb_fm"), [16, T], BF16))
        srow = E(nc.sbuf_tensor(host.u("srow"), [1, 512], F32))
        murow = E(nc.sbuf_tensor(host.u("murow"), [1, 512], F32))
        e2row = E(nc.sbuf_tensor(host.u("e2row"), [1, 512], F32))
        sq16 = E(nc.sbuf_tensor(host.u("sq16"), [16, 512], BF16))
        mu_col = E(nc.sbuf_tensor(host.u("mu_col"), [128, 32], F32))
        rs_col = E(nc.sbuf_tensor(host.u("rs_col"), [128, 32], F32))
        a_sb = E(nc.sbuf_tensor(host.u("a_sb"), [128, DI], F32))
        d2_sb = E(nc.sbuf_tensor(host.u("d2_sb"), [128, DI], F32))
        sz_sb = [E(nc.sbuf_tensor(host.u(f"szsb{i}"), [128, DI], BF16)) for i in range(2)]
        yns_fm = E(nc.sbuf_tensor(host.u("yns_fm"), [128, 16 * 128], BF16))
        xin = [E(nc.sbuf_tensor(host.u(f"xin{i}"), [128, D], F32)) for i in range(2)]
        out_sb = [E(nc.sbuf_tensor(host.u(f"outsb{i}"), [128, D], F32)) for i in range(2)]
        lnt32 = a_sb[:, 0:1024]
        sqacc = a_sb[:, 1024:2048]
        zrow = d2_sb[0:1, 0:2048]
        xmrow = srow
        dmrow = murow
        dcrow = e2row
        rep_sb = ln_fm[0:16, 0:DI]
        o16c_bf = o16cb2
        s1 = E(nc.sbuf_tensor(host.u("s1"), [128, 1], F32))
        ssq = E(nc.sbuf_tensor(host.u("ssq"), [128, 1], F32))
        mean = E(nc.sbuf_tensor(host.u("mean"), [128, 1], F32))
        m2t = E(nc.sbuf_tensor(host.u("m2t"), [128, 1], F32))
        var = E(nc.sbuf_tensor(host.u("var"), [128, 1], F32))
        lnv = E(nc.sbuf_tensor(host.u("lnv"), [128, 1], F32))
        rstd = E(nc.sbuf_tensor(host.u("rstd"), [128, 1], F32))
        ps_a = [E(nc.psum_tensor(host.u(f"psa{i}"), [128, 512], F32)) for i in range(2)]
        ps_xp = E(nc.psum_tensor(host.u("psxp"), [64, 512], F32))
        ps_b16 = E(nc.psum_tensor(host.u("psb16"), [16, 512], F32))
        ps_n = E(nc.psum_tensor(host.u("psn"), [1, 512], F32))
        ps_y = E(nc.psum_tensor(host.u("psy"), [128, 1024], F32))
        s_ld = host.sem("s_ld", serial=True)
        s_w = host.sem("s_w", serial=True)
        s_cast = host.sem("s_cast", serial=True)
        s_st = host.sem("s_st", serial=True)
        s_xs = host.sem("s_xs", serial=True)
        s_pe = host.sem("s_pe")
        s_ape = host.sem("s_ape")
        s_ype = host.sem("s_ype")

        cin = [a_sb[:], d2_sb[:]]
        cout = [xs2[:, 0:4096], xsh[:, 0:4096]]
        cast_jobs = []
        for fc in range(8):
            for half, scale in ((0, None), (1, 0.5)):
                cast_jobs.append((
                    wi_d.rearrange("(c pp) m -> pp c m", pp=128)[:, fc:fc + 1,
                                                               half * DI:(half + 1) * DI],
                    wib_d.rearrange("(c pp) m -> pp c m", pp=128)[:, fc:fc + 1,
                                                                 half * DI:(half + 1) * DI],
                    2048, scale))
        for hq in range(8):
            cast_jobs.append((
                wo_d.rearrange("(c pp) m -> pp c m", pp=128)[:, hq * 2:(hq + 1) * 2, :],
                wob_d.rearrange("(c pp) m -> pp c m", pp=128)[:, hq * 2:(hq + 1) * 2, :],
                2048, None))
        cast_jobs = host.filter_cast("attn1", cast_jobs)
        cast_v = {}
        cast_base = s_cast.n
        for i, (src, dst, cols, scale) in enumerate(cast_jobs):
            b = i % 2
            nch = src.shape[1]
            waits = []
            if i >= 2:
                waits.append((s_cast, cast_base + (i - 1) * 16))
                waits.append((sv, cast_v[i - 2]))
            ld_v = p.emit("sync", lambda e, s=src, b=b, c=cols, nch=nch: e.dma_start(
                out=cin[b][:, 0:c].rearrange("p (c n) -> p c n", c=nch), in_=s),
                waits=waits, inc=s_w, inc_n=16)
            if scale is None:
                cv = p.emit("vector", lambda e, b=b, c=cols: e.tensor_copy(
                    cout[b][:, 0:c], cin[b][:, 0:c]), waits=[(s_w, ld_v)])
            else:
                cv = p.emit("vector", lambda e, b=b, c=cols, sc=scale: e.tensor_scalar(
                    out=cout[b][:, 0:c], in0=cin[b][:, 0:c], scalar1=sc,
                    scalar2=None, op0=ALU.mult), waits=[(s_w, ld_v)])
            cast_v[i] = cv
            p.emit("gpsimd", lambda e, d=dst, b=b, c=cols, nch=nch: e.dma_start(
                out=d, in_=cout[b][:, 0:c].rearrange("p (c n) -> p c n", c=nch)),
                waits=[(sv, cv)], inc=s_cast, inc_n=16)
        cast_done = host.cast_milestone("attn1", s_cast)
        lv1 = p.emit("sync", lambda e: e.dma_start(
            out=cin[0][:, 0:16 * 96].rearrange("p (c n) -> p c n", c=16),
            in_=wx_d.rearrange("(c pp) m -> pp c m", pp=128)),
            waits=[(s_cast, cast_done)], inc=s_w, inc_n=16)
        wxc = p.emit("vector", lambda e: e.tensor_copy(wx_sb[:], cin[0][:, 0:16 * 96]),
                     waits=[(s_w, lv1), (sv, cast_v[len(cast_jobs) - 2])])
        lv2 = p.emit("sync", lambda e: e.dma_start(
            out=cin[1][:, 0:DI][0:64, :], in_=wdt_d), inc=s_w, inc_n=16)
        wdc = p.emit("vector", lambda e: e.tensor_copy(wdt_sb[:], cin[1][0:64, 0:DI]),
                     waits=[(s_w, lv2), (sv, cast_v[len(cast_jobs) - 1])])
        lv3 = p.emit("sync", lambda e: e.dma_start(
            out=wc_sb[:].rearrange("p (c n) -> p c n", c=16),
            in_=wc_d.rearrange("(c pp) o k -> pp c (o k)", pp=128),
            allow_slow_non_contiguous=True), inc=s_w, inc_n=16)
        wcc = p.emit("vector", lambda e: e.tensor_scalar(
            out=wc_sb[:], in0=wc_sb[:], scalar1=0.5, scalar2=None, op0=ALU.mult),
            waits=[(s_w, lv3)])
        lv4 = p.emit("sync", lambda e: e.dma_start(
            out=bdt_sb[:], in_=bdt_d.rearrange("(c p) -> p c", p=128),
            allow_slow_non_contiguous=True), inc=s_w, inc_n=16)
        consts_w = s_w.n
        pso_free = 0
        for which, row_d, dst in ((0, lng_d, g_bc), (1, lnb_d, b_bc)):
            ldv = p.emit("sync", lambda e, rd2=row_d: e.dma_start(
                out=zrow[:], in_=rd2.rearrange("(o n) -> o n", o=1)),
                waits=[(sv, pso_free), (s_cast, cast_done), (sv, wdc)],
                inc=s_w, inc_n=16)
            for c in range(4):
                bg = p.emit("tensor", lambda e, c=c: e.matmul(
                    ps_a[0][:], o1r[:], zrow[0:1, c * 512:(c + 1) * 512],
                    start=True, stop=True),
                    waits=[(s_w, max(ldv, consts_w)), (sv, pso_free)], inc=s_pe)
                pso_free = p.emit("vector", lambda e, c=c, dst=dst: e.tensor_copy(
                    dst[:, c * 512:(c + 1) * 512], ps_a[0][:]),
                    waits=[(s_pe, bg)])
        gb_done = pso_free

        xin_last_read = [0, 0]
        psa_free = [pso_free, 0]
        psy_free_sv = 0
        psn_free = 0
        psb16_free = 0
        psxp_free = 0
        xs_store_v = {}
        sz_store_v = {}

        def emit_ln_tg(tg, first):
            nonlocal psy_free_sv
            for t4 in range(4):
                gt = tg * 4 + t4
                b = gt % 2
                ldw = [(sv, xin_last_read[b])]
                if first and t4 == 0:
                    ldw += [(sv, wxc), (sv, wdc), (sv, gb_done)]
                xv = p.emit("gpsimd", lambda e, gt=gt, b=b: e.dma_start(
                    out=xin[b][:], in_=x_d[gt * 128:(gt + 1) * 128, :]),
                    waits=ldw, inc=s_ld, inc_n=16)
                sumv = p.emit("vector", lambda e, b=b: e.reduce_sum(
                    s1[:], xin[b][:], axis=AX.X), waits=[(s_ld, xv)])
                qv = p.emit("scalar", lambda e, b=b: e.activation(
                    out=sqacc, in_=xin[b][:], func=AF.Square, accum_out=ssq[:]),
                    waits=[(s_ld, xv)])
                p.emit("vector", lambda e: e.tensor_scalar(
                    out=mean[:], in0=s1[:], scalar1=1.0 / D, scalar2=None, op0=ALU.mult),
                    waits=[(sv, sumv)])
                p.emit("vector", lambda e: e.tensor_scalar(
                    out=var[:], in0=ssq[:], scalar1=1.0 / D, scalar2=None, op0=ALU.mult),
                    waits=[(sa, qv)])
                p.emit("vector", lambda e: e.tensor_mul(m2t[:], mean[:], mean[:]))
                p.emit("vector", lambda e: e.tensor_sub(var[:], var[:], m2t[:]))
                vv = p.emit("vector", lambda e: e.tensor_scalar(
                    out=var[:], in0=var[:], scalar1=1e-5, scalar2=None, op0=ALU.add))
                lnv_ = p.emit("scalar", lambda e: e.activation(
                    out=lnv[:], in_=var[:], func=AF.Ln), waits=[(sv, vv)])
                nh2 = p.emit("vector", lambda e: e.tensor_scalar(
                    out=lnv[:], in0=lnv[:], scalar1=-0.5, scalar2=None, op0=ALU.mult),
                    waits=[(sa, lnv_)])
                rv = p.emit("scalar", lambda e: e.activation(
                    out=rstd[:], in_=lnv[:], func=AF.Exp), waits=[(sv, nh2)])
                l32 = p.emit("vector", lambda e, b=b: e.tensor_scalar(
                    out=lnt32, in0=xin[b][:], scalar1=mean[:], scalar2=rstd[:],
                    op0=ALU.subtract, op1=ALU.mult), waits=[(sa, rv)])
                xin_last_read[b] = l32
                tw = [(sv, l32), (s_w, consts_w), (sv, psy_free_sv)]
                if first and t4 == 0:
                    tw.append((s_cast, cast_done))
                tv = None
                for c in range(8):
                    tv = p.emit("tensor", lambda e, c=c: e.transpose(
                        ps_y[:, c * 128:(c + 1) * 128],
                        lnt32[:, c * 128:(c + 1) * 128], identf[:]),
                        waits=tw if c == 0 else (), inc=s_pe if c == 7 else None)
                psy_free_sv = p.emit("vector", lambda e, t4=t4: e.tensor_copy(
                    ln_fm[:].rearrange("p (c n) -> p c n", c=8)[:, :, t4 * 128:(t4 + 1) * 128],
                    ps_y[:].rearrange("p (c n) -> p c n", c=8)),
                    waits=[(s_pe, tv)])
            return sv.n

        wi_v = p.emit("sync", lambda e: e.dma_start(
            out=wi_sb[:].rearrange("p (c n) -> p c n", c=8),
            in_=wib_d.rearrange("(c pp) m -> pp c m", pp=128)[:, :, 0:DI]),
            waits=[(s_cast, cast_done)], inc=s_w, inc_n=16)
        for tg in range(NTG):
            ln_done = emit_ln_tg(tg, tg == 0)
            for c in range(16):
                pp = c % 2
                mats = []
                for k in range(8):
                    mats.append(lambda e, c=c, k=k: e.matmul(
                        ps_a[c % 2][:],
                        wi_sb[:, k * 2048 + c * 128: k * 2048 + (c + 1) * 128],
                        ln_fm[:, k * 512:(k + 1) * 512],
                        start=(k == 0), stop=(k == 7)))
                gvv = p.mm_group(mats, waits=[(sv, ln_done), (sv, psa_free[pp]),
                                              (s_w, wi_v)], inc=s_ape)
                ev = p.emit("vector", lambda e, pp=pp: e.tensor_copy(
                    xsh[:, 0:512], ps_a[pp][:]),
                    waits=[(s_ape, gvv), (s_xs, s_xs.n)])
                psa_free[pp] = ev
                p.emit("gpsimd", lambda e, c=c, tg=tg: e.dma_start(
                    out=xsr_d.rearrange("(c pp) t -> pp c t", pp=128)[:, c:c + 1,
                                                                     tg * 512:(tg + 1) * 512],
                    in_=xsh[:, 0:512].rearrange("p (o n) -> p o n", o=1)),
                    waits=[(sv, ev)], inc=s_xs, inc_n=16)
        xs_pass_pe = s_ape.n
        wi_z_v = p.emit("sync", lambda e: e.dma_start(
            out=wi_sb[:].rearrange("p (c n) -> p c n", c=8),
            in_=wib_d.rearrange("(c pp) m -> pp c m", pp=128)[:, :, DI:2 * DI]),
            waits=[(s_ape, xs_pass_pe)], inc=s_w, inc_n=16)
        for tg in range(NTG):
            ln_done = emit_ln_tg(tg, False)
            for t4 in range(4):
                for zc in range(4):
                    pp = zc % 2
                    mats = []
                    for k in range(8):
                        mats.append(lambda e, t4=t4, zc=zc, k=k: e.matmul(
                            ps_a[zc % 2][:],
                            ln_fm[:, k * 512 + t4 * 128: k * 512 + (t4 + 1) * 128],
                            wi_sb[:, k * 2048 + zc * 512: k * 2048 + (zc + 1) * 512],
                            start=(k == 0), stop=(k == 7)))
                    gvv = p.mm_group(mats, waits=[(sv, ln_done), (sv, psa_free[pp]),
                                                  (s_w, wi_z_v)], inc=s_ape)
                    tnv = p.emit("scalar", lambda e, pp=pp: e.activation(
                        out=t_sb[:], in_=ps_a[pp][:], func=AF.Tanh),
                        waits=[(s_ape, gvv), (sv, sv.n)])
                    szv = p.emit("vector", lambda e, pp=pp, zc=zc: e.scalar_tensor_tensor(
                        out=d2_sb[:].bitcast(BF16)[:, zc * 512:(zc + 1) * 512],
                        in0=t_sb[:], scalar=1.0, in1=ps_a[pp][:],
                        op0=ALU.add, op1=ALU.mult),
                        waits=[(sa, tnv), (s_xs, s_xs.n)])
                    psa_free[pp] = szv
                p.emit("gpsimd", lambda e, tg=tg, t4=t4: e.dma_start(
                    out=sz_d[(tg * 4 + t4) * 128:(tg * 4 + t4) * 128 + 128, :],
                    in_=d2_sb[:].bitcast(BF16)[:, 0:DI]),
                    waits=[(sv, sv.n)], inc=s_xs, inc_n=16)
            xs_store_v[tg] = s_xs.n

        dcol_done = 0
        for tg in range(NTG):
            lo = max(0, tg * 512 - 1)
            hi = min(T, tg * 512 + 513)
            width = hi - lo
            off = 1 if tg == 0 else 0
            ldh = p.emit("gpsimd", lambda e, lo=lo, width=width, off=off: e.dma_start(
                out=xsh[:].rearrange("p (c n) -> p c n", c=16)[:, :, off:off + width],
                in_=xsr_d.rearrange("(c pp) t -> pp c t", pp=128)[:, :, lo:lo + width]),
                waits=[(s_xs, xs_store_v[NTG - 1]), (sv, sv.n)], inc=s_xs, inc_n=16)
            zpad = 0
            if tg == 0:
                zpad = p.emit("vector", lambda e: e.memset(
                    xsh[:].rearrange("p (c n) -> p c n", c=16)[:, :, 0:1], 0.0))
            if tg == NTG - 1:
                zpad = p.emit("vector", lambda e: e.memset(
                    xsh[:].rearrange("p (c n) -> p c n", c=16)[:, :, 513:514], 0.0))
            for c in range(16):
                cw = [(s_xs, s_xs.n), (sa, sa.n)]
                c1 = p.emit("vector", lambda e, c=c: e.tensor_scalar(
                    out=xc_sb[:], in0=xsh[:, c * 514 + 0: c * 514 + 512],
                    scalar1=wc_sb[:, c * 3:c * 3 + 1], scalar2=None, op0=ALU.mult),
                    waits=cw + [(sv, wcc)])
                c2 = p.emit("vector", lambda e, c=c: e.scalar_tensor_tensor(
                    out=xc_sb[:], in0=xsh[:, c * 514 + 1: c * 514 + 513],
                    scalar=wc_sb[:, c * 3 + 1:c * 3 + 2], in1=xc_sb[:],
                    op0=ALU.mult, op1=ALU.add))
                c3 = p.emit("vector", lambda e, c=c: e.scalar_tensor_tensor(
                    out=xc_sb[:], in0=xsh[:, c * 514 + 2: c * 514 + 514],
                    scalar=wc_sb[:, c * 3 + 2:c * 3 + 3], in1=xc_sb[:],
                    op0=ALU.mult, op1=ALU.add))
                tnv = p.emit("scalar", lambda e: e.activation(
                    out=t_sb[:], in_=xc_sb[:], func=AF.Tanh),
                    waits=[(sv, c3), (sv, sv.n)])
                p.emit("vector", lambda e, c=c: e.scalar_tensor_tensor(
                    out=xs2[:, c * 512:(c + 1) * 512], in0=t_sb[:], scalar=1.0,
                    in1=xc_sb[:], op0=ALU.add, op1=ALU.mult),
                    waits=[(sa, tnv)])
            xs2_done = sv.n
            mats = []
            for c in range(16):
                mats.append(lambda e, c=c: e.matmul(
                    ps_n[:], o128c[:], xs2[:, c * 512:(c + 1) * 512],
                    start=(c == 0), stop=(c == 15)))
            gn = p.mm_group(mats, waits=[(sv, xs2_done), (sv, psn_free),
                                         (s_w, consts_w)], inc=s_pe)
            psn_free = p.emit("vector", lambda e: e.tensor_scalar(
                out=xmrow[:], in0=ps_n[0:1, :], scalar1=1.0 / DI, scalar2=None,
                op0=ALU.mult), waits=[(s_pe, gn)])
            mats = []
            for c in range(16):
                mats.append(lambda e, c=c: e.matmul(
                    ps_xp[:], wx_sb[:, c * 96: c * 96 + 64],
                    xs2[:, c * 512:(c + 1) * 512],
                    start=(c == 0), stop=(c == 15)))
            gx = p.mm_group(mats, waits=[(sv, xs2_done), (sv, psxp_free), (sv, wxc)],
                            inc=s_pe)
            psxp_free = p.emit("vector", lambda e: e.tensor_copy(
                xp_sb[:], ps_xp[:]), waits=[(s_pe, gx)])
            mats = []
            for c in range(16):
                mats.append(lambda e, c=c: e.matmul(
                    ps_b16[:], wx_sb[:, c * 96 + 64: c * 96 + 80],
                    xs2[:, c * 512:(c + 1) * 512],
                    start=(c == 0), stop=(c == 15)))
            gb = p.mm_group(mats, waits=[(sv, psb16_free)], inc=s_pe)
            psb16_free = p.emit("vector", lambda e: e.tensor_copy(
                bp_sb[:], ps_b16[:]), waits=[(s_pe, gb)])
            mats = []
            for c in range(16):
                mats.append(lambda e, c=c: e.matmul(
                    ps_b16[:], wx_sb[:, c * 96 + 80: c * 96 + 96],
                    xs2[:, c * 512:(c + 1) * 512],
                    start=(c == 0), stop=(c == 15)))
            gc = p.mm_group(mats, waits=[(sv, psb16_free)], inc=s_pe)
            psb16_free = p.emit("vector", lambda e, tg=tg: e.tensor_copy(
                yb_fm[:, tg * 512:(tg + 1) * 512], ps_b16[:]), waits=[(s_pe, gc)])
            dt_n = None
            p1prev = 0
            for oc in range(16):
                gd = p.emit("tensor", lambda e, oc=oc: e.matmul(
                    ps_a[0][:], wdt_sb[0:64, oc * 128:(oc + 1) * 128], xp_sb[:],
                    start=True, stop=True),
                    waits=[(sv, psxp_free), (sv, psa_free[0]), (sv, wdc)], inc=s_ape)
                e1v = p.emit("scalar", lambda e, oc=oc: e.activation(
                    out=e1_sb[:], in_=ps_a[0][:], func=AF.Exp,
                    bias=bdt_sb[:, oc:oc + 1]),
                    waits=[(s_ape, gd), (s_w, lv4), (sv, p1prev),
                           (s_pe, dt_n if dt_n else 0)])
                psa_free[0] = e1v
                p1 = p.emit("vector", lambda e: e.tensor_scalar(
                    out=e1_sb[:], in0=e1_sb[:], scalar1=1.0, scalar2=None, op0=ALU.add),
                    waits=[(sa, e1v)])
                p1prev = p1
                spv = p.emit("scalar", lambda e: e.activation(
                    out=sp_bf[:], in_=e1_sb[:], func=AF.Ln), waits=[(sv, p1)])
                dt_n = p.emit("tensor", lambda e, oc=oc: e.matmul(
                    ps_n[:], o128c[:], sp_bf[:],
                    start=(oc == 0), stop=(oc == 15)),
                    waits=[(sa, spv), (sv, psn_free)] if oc == 0 else [(sa, spv)],
                    inc=s_pe)
            nneg = p.emit("vector", lambda e: e.tensor_scalar(
                out=dmrow[:], in0=ps_n[0:1, :], scalar1=-1.0 / DI, scalar2=None,
                op0=ALU.mult), waits=[(s_pe, dt_n)])
            psn_free = nneg
            dca = p.emit("scalar", lambda e: e.activation(
                out=dcrow[:], in_=dmrow[:], func=AF.Exp), waits=[(sv, nneg)])
            gdc = p.emit("tensor", lambda e: e.matmul(
                ps_b16[:], o16r[:], dcrow[:], start=True, stop=True),
                waits=[(sa, dca), (sv, psb16_free)], inc=s_pe)
            psb16_free = p.emit("vector", lambda e, tg=tg: e.tensor_copy(
                dec_fm[:, tg * 512:(tg + 1) * 512], ps_b16[:]), waits=[(s_pe, gdc)])
            gxm = p.emit("tensor", lambda e: e.matmul(
                ps_b16[:], o16r[:], xmrow[:], start=True, stop=True),
                waits=[(sv, psn_free), (sv, psb16_free)], inc=s_pe)
            psb16_free = p.emit("vector", lambda e, tg=tg: e.tensor_mul(
                bx_fm[:, tg * 512:(tg + 1) * 512], bp_sb[:], ps_b16[:]),
                waits=[(s_pe, gxm)])
        a2_done = sv.n

        wo_lo_v = p.emit("sync", lambda e: e.dma_start(
            out=xs2[:].rearrange("p (c n) -> p c n", c=8),
            in_=wob_d.rearrange("(c pp) m -> pp c m", pp=128)[:, 0:8, :]),
            waits=[(sv, a2_done), (s_pe, s_pe.n), (s_ape, s_ape.n)],
            inc=s_w, inc_n=16)
        wo_hi_v = p.emit("sync", lambda e: e.dma_start(
            out=xsh[:, 0:8192].rearrange("p (c n) -> p c n", c=8),
            in_=wob_d.rearrange("(c pp) m -> pp c m", pp=128)[:, 8:16, :]),
            inc=s_w, inc_n=16)
        wo_ld = p.emit("sync", lambda e: e.dma_start(out=rep_sb, in_=rep_d),
                       inc=s_w, inc_n=16)

        def wo_slice(c, nh):
            base = xs2 if c < 8 else xsh
            cc = c % 8
            return base[:, cc * 1024 + nh * 512: cc * 1024 + (nh + 1) * 512]

        scv = p.emit("vector", lambda e: e.tensor_tensor_scan(
            out=h_fm[:], data0=dec_fm[:], data1=bx_fm[:], initial=0.0,
            op0=ALU.mult, op1=ALU.add), waits=[(sv, a2_done)])
        for c in range(8):
            gs = p.emit("tensor", lambda e, c=c: e.matmul(
                ps_n[:], o16c[:], h_fm[:, c * 512:(c + 1) * 512],
                start=True, stop=True),
                waits=[(sv, scv), (sv, psn_free)], inc=s_pe)
            psn_free = p.emit("vector", lambda e: e.tensor_copy(
                srow[:], ps_n[0:1, :]), waits=[(s_pe, gs)])
            gsb = p.emit("tensor", lambda e: e.matmul(
                ps_b16[:], o16r[:], srow[:], start=True, stop=True),
                waits=[(sv, psn_free), (sv, psb16_free)], inc=s_pe)
            ybv = p.emit("vector", lambda e, c=c: e.tensor_mul(
                yb_fm[:, c * 512:(c + 1) * 512], yb_fm[:, c * 512:(c + 1) * 512],
                ps_b16[:]), waits=[(s_pe, gsb)])
            psb16_free = ybv
            gm = p.emit("tensor", lambda e, c=c: e.matmul(
                ps_n[:], o16c_bf[:], yb_fm[:, c * 512:(c + 1) * 512],
                start=True, stop=True),
                waits=[(sv, ybv), (sv, psn_free)], inc=s_pe)
            psn_free = p.emit("vector", lambda e: e.tensor_scalar(
                out=murow[:], in0=ps_n[0:1, :], scalar1=1.0 / 16, scalar2=None,
                op0=ALU.mult), waits=[(s_pe, gm)])
            sqv = p.emit("scalar", lambda e, c=c: e.activation(
                out=sq16[:], in_=yb_fm[:, c * 512:(c + 1) * 512], func=AF.Square),
                waits=[(sv, ybv)])
            gq2 = p.emit("tensor", lambda e: e.matmul(
                ps_n[:], o16c_bf[:], sq16[:], start=True, stop=True),
                waits=[(sa, sqv), (sv, psn_free)], inc=s_pe)
            psn_free = p.emit("vector", lambda e: e.tensor_scalar(
                out=e2row[:], in0=ps_n[0:1, :], scalar1=1.0 / 16, scalar2=None,
                op0=ALU.mult), waits=[(s_pe, gq2)])
            p.emit("vector", lambda e: e.tensor_mul(srow[:], murow[:], murow[:]))
            p.emit("vector", lambda e: e.tensor_sub(e2row[:], e2row[:], srow[:]))
            vv3 = p.emit("vector", lambda e: e.tensor_scalar(
                out=e2row[:], in0=e2row[:], scalar1=1e-5, scalar2=None, op0=ALU.add))
            lvx = p.emit("scalar", lambda e: e.activation(
                out=e2row[:], in_=e2row[:], func=AF.Ln), waits=[(sv, vv3)])
            vv4 = p.emit("vector", lambda e: e.tensor_scalar(
                out=e2row[:], in0=e2row[:], scalar1=-0.5, scalar2=None, op0=ALU.mult),
                waits=[(sa, lvx)])
            rsx = p.emit("scalar", lambda e: e.activation(
                out=e2row[:], in_=e2row[:], func=AF.Exp), waits=[(sv, vv4)])
            for j in range(4):
                tt = c * 4 + j
                g1 = p.emit("tensor", lambda e, j=j: e.transpose(
                    ps_y[:, 0:1], murow[0:1, j * 128:(j + 1) * 128], identf[0:1, 0:1]),
                    waits=[(sv, psy_free_sv), (sa, rsx)] if j == 0 else
                          [(sv, psy_free_sv)],
                    inc=s_pe)
                psy_free_sv = p.emit("vector", lambda e, tt=tt: e.tensor_copy(
                    mu_col[:, tt:tt + 1], ps_y[:, 0:1]), waits=[(s_pe, s_pe.n)])
                g2 = p.emit("tensor", lambda e, j=j: e.transpose(
                    ps_y[:, 0:1], e2row[0:1, j * 128:(j + 1) * 128], identf[0:1, 0:1]),
                    waits=[(sv, psy_free_sv)], inc=s_pe)
                psy_free_sv = p.emit("vector", lambda e, tt=tt: e.tensor_copy(
                    rs_col[:, tt:tt + 1], ps_y[:, 0:1]), waits=[(s_pe, s_pe.n)])

        out_last_store = [0, 0]
        sz_ld = [0, 0]
        d2_free = [0]
        for tt in range(NTT):
            b = tt % 2
            szv = p.emit("gpsimd", lambda e, tt=tt, b=b: e.dma_start(
                out=sz_sb[b][:], in_=sz_d[tt * 128:(tt + 1) * 128, :]),
                waits=[(sv, sz_ld[b])], inc=s_xs, inc_n=16)
            d2v = None
            for half in range(2):
                mats = []
                for nh in range(2):
                    mats.append(lambda e, tt=tt, half=half, nh=nh: e.matmul(
                        ps_y[:, nh * 512:(nh + 1) * 512],
                        yb_fm[:, tt * 128:(tt + 1) * 128],
                        rep_sb[:, half * 1024 + nh * 512: half * 1024 + (nh + 1) * 512],
                        start=True, stop=True))
                gy = p.mm_group(mats, waits=[(sv, psy_free_sv), (s_w, wo_ld)],
                                inc=s_ype)
                av = p.emit("vector", lambda e, tt=tt, half=half: e.tensor_scalar(
                    out=a_sb[:, half * 1024:(half + 1) * 1024], in0=ps_y[:],
                    scalar1=mu_col[:, tt:tt + 1], scalar2=rs_col[:, tt:tt + 1],
                    op0=ALU.subtract, op1=ALU.mult),
                    waits=[(s_ype, gy)])
                psy_free_sv = av
                bv2 = p.emit("vector", lambda e, half=half: e.tensor_mul(
                    a_sb[:, half * 1024:(half + 1) * 1024],
                    a_sb[:, half * 1024:(half + 1) * 1024],
                    g_bc[:, half * 1024:(half + 1) * 1024]),
                    waits=[(sv, gb_done)])
                cv2 = p.emit("vector", lambda e, half=half: e.tensor_add(
                    a_sb[:, half * 1024:(half + 1) * 1024],
                    a_sb[:, half * 1024:(half + 1) * 1024],
                    b_bc[:, half * 1024:(half + 1) * 1024]))
                d2v = p.emit("vector", lambda e, half=half, b=b: e.tensor_mul(
                    d2_sb[:, half * 1024:(half + 1) * 1024],
                    a_sb[:, half * 1024:(half + 1) * 1024],
                    sz_sb[b][:, half * 1024:(half + 1) * 1024]),
                    waits=[(s_xs, szv), (s_pe, d2_free[0])])
            sz_ld[b] = d2v
            tcount = 0
            for r4 in range(4):
                pp = r4 % 2
                tv = None
                for j in range(4):
                    c = r4 * 4 + j
                    tv = p.emit("tensor", lambda e, c=c, j=j, pp=pp: e.transpose(
                        ps_a[pp][:, j * 128:(j + 1) * 128],
                        d2_sb[:, c * 128:(c + 1) * 128], identf[:]),
                        waits=[(sv, d2v), (sv, psa_free[pp])] if j == 0 else (),
                        inc=s_pe if j == 3 else None)
                psa_free[pp] = p.emit("vector", lambda e, r4=r4, pp=pp: e.tensor_copy(
                    yns_fm[:, r4 * 512:(r4 + 1) * 512], ps_a[pp][:]),
                    waits=[(s_pe, tv)])
            d2_free[0] = s_pe.n
            yns_done = sv.n
            mats = []
            for nh in range(2):
                for c in range(16):
                    mats.append(lambda e, c=c, nh=nh: e.matmul(
                        ps_y[:, nh * 512:(nh + 1) * 512],
                        yns_fm[:, c * 128:(c + 1) * 128],
                        wo_slice(c, nh),
                        start=(c == 0), stop=(c == 15)))
            go = p.mm_group(mats, waits=[(sv, yns_done), (sv, psy_free_sv),
                                         (s_w, wo_ld)], inc=s_ype)
            xv = p.emit("gpsimd", lambda e, tt=tt, b=b: e.dma_start(
                out=xin[b][:], in_=x_d[tt * 128:(tt + 1) * 128, :]),
                waits=[(sv, xin_last_read[b])], inc=s_ld, inc_n=16)
            rv2 = p.emit("vector", lambda e, b=b: e.tensor_add(
                out_sb[b][:], xin[b][:], ps_y[:]),
                waits=[(s_ld, xv), (s_ype, go), (s_st, out_last_store[b])])
            xin_last_read[b] = rv2
            psy_free_sv = rv2
            out_last_store[b] = p.emit("gpsimd", lambda e, tt=tt, b=b: e.dma_start(
                out=xo_d[tt * 128:(tt + 1) * 128, :], in_=out_sb[b][:]),
                waits=[(sv, rv2)], inc=s_st, inc_n=16)

        p.wait("gpsimd", [(s_st, s_st.n)])
    if own:
        return host.finish()
    return None


def build_fused():
    host = Host(standalone=False)
    nc = host.nc
    host._n_sections = 6
    for k in _CONST_DEFS:
        if k not in ("ecst512", "iota128", "tri128", "sel127"):
            host.const(k)
    x0 = nc.dram_tensor("x", [T, D], F32, kind="ExternalInput").ap()
    xo = nc.dram_tensor("xo", [T, D], F32, kind="ExternalOutput").ap()
    host._x_bufs = [host.scratch("xbuf0", [T, D], F32),
                    host.scratch("xbuf1", [T, D], F32)]
    host._x_cur = x0
    host._xo_final = xo
    build_ffn0(host=host)
    build_attn0(host=host)
    build_ffn0(host=host)
    build_ffn1(host=host)
    build_attn1(host=host)
    build_ffn1(host=host)
    return host.finish()


_CACHE = {}


def _get(name):
    if name not in _CACHE:
        _CACHE[name] = {"ffn0": build_ffn0, "attn0": build_attn0,
                        "ffn1": build_ffn1, "attn1": build_attn1,
                        "fused": build_fused}[name]()
    return _CACHE[name]


def kernel(**w):
    f32 = lambda k: np.ascontiguousarray(np.asarray(w[k], dtype=np.float32))
    x = f32("L0_attn_x")
    base = {
        "f0_r": f32("L0_ffn_w_router"), "f0_w1": f32("L0_ffn_w1"),
        "f0_b1": f32("L0_ffn_b1"), "f0_w2": f32("L0_ffn_w2"),
        "f0_b2": f32("L0_ffn_b2"),
        "a0_wq": f32("L0_attn_Wq"), "a0_bq": f32("L0_attn_bq"),
        "a0_wk": f32("L0_attn_Wk"), "a0_bk": f32("L0_attn_bk"),
        "a0_wv": f32("L0_attn_Wv"), "a0_bv": f32("L0_attn_bv"),
        "a0_wl": f32("L0_attn_Wl"), "a0_bl": f32("L0_attn_bl"),
        "a0_wo": f32("L0_attn_Wo"), "a0_bo": f32("L0_attn_bo"),
        "f1_r": f32("L1_ffn_router_w"), "f1_gw": f32("L1_ffn_gate_w"),
        "f1_uw": f32("L1_ffn_up_w"), "f1_dw": f32("L1_ffn_down_w"),
        "f1_sg": f32("L1_ffn_shared_gate_w"), "f1_su": f32("L1_ffn_shared_up_w"),
        "f1_sd": f32("L1_ffn_shared_down_w"),
        "a1_wi": f32("L1_attn_W_in"), "a1_wc": f32("L1_attn_W_conv"),
        "a1_wx": f32("L1_attn_W_xproj"), "a1_wdt": f32("L1_attn_W_dt"),
        "a1_bdt": f32("L1_attn_b_dt"), "a1_wo": f32("L1_attn_W_out"),
        "a1_lng": f32("L1_attn_ln_g"), "a1_lnb": f32("L1_attn_ln_b"),
    }
    nc = _get("fused")
    in_maps = [dict(base, x=x[c]) for c in range(N_CORES)]
    res = run_bass_kernel_spmd(nc, in_maps, core_ids=list(range(N_CORES)))
    return np.stack([r["xo"] for r in res.results])
